# version 01
# baseline (speedup 1.0000x reference)
import math
import contextlib
from collections import defaultdict

import numpy as np
import ml_dtypes

import concourse.bass as bass
import concourse.mybir as mybir
from concourse.bass_utils import run_bass_kernel_spmd

F32 = mybir.dt.float32
BF16 = mybir.dt.bfloat16
I32 = mybir.dt.int32
AF = mybir.ActivationFunctionType
ALU = mybir.AluOpType
AX = mybir.AxisListType

B_SZ = 4
SEQ = 4096
D = 2048
DC = 16
H = 16
DH = 128
NE = 16
FM = 1024
FSH = 2048
TILE = 1024
NSLOT = 16
KSCALE = 1.0 / math.sqrt(DH)
NEG = -1.0e30

NQROT = 4
STREAM_OF = {'T': 'T', 'V': 'V', 'S': 'S', 'P': 'P', 'Y': 'Y'}
STEP = {'T': 1, 'V': 1, 'S': 1, 'P': 1, 'Y': 1}
for _i in range(NQROT):
    STREAM_OF[f'QP{_i}'] = 'P'
    STREAM_OF[f'QY{_i}'] = 'Y'
    STEP[f'QP{_i}'] = 16
    STEP[f'QY{_i}'] = 16
STREAMS = ['T', 'V', 'S', 'P', 'Y']
SEMKEYS = list(STEP)


def slopes():
    return [0.5 ** ((h + 1) / 2.0) for h in range(H)]


def slot_global(j, half):
    cls = j // 2
    m = j % 2
    return 4 * cls + (3 * m if half == 0 else 1 + m)


class Bld:

    def __init__(self, nc, es):
        self.nc = nc
        self.sems = {k: es.enter_context(nc.semaphore("sem" + k)) for k in SEMKEYS}
        self.prog = {s: [] for s in STREAMS}
        self.cnt = defaultdict(int)
        self.know = {s: defaultdict(int) for s in STREAMS}
        self.snap = {k: [] for k in SEMKEYS}
        self.lastw = {}
        self.readers = defaultdict(dict)
        self.nwaits = 0
        self.qrr = {'P': 0, 'Y': 0}

    def _deps_for(self, reads, writes):
        deps = {}
        def add(k, c):
            if deps.get(k, 0) < c:
                deps[k] = c
        for r in reads:
            lw = self.lastw.get(r)
            if lw:
                add(*lw)
        for w in writes:
            lw = self.lastw.get(w)
            if lw:
                add(*lw)
            for k, c in self.readers.get(w, {}).items():
                add(k, c)
        return deps

    def emit(self, E, fname, *args, reads=(), writes=(), extra_deps=(), **kwargs):
        stream = STREAM_OF[E]
        deps = self._deps_for(reads, writes)
        for k, c in extra_deps:
            if deps.get(k, 0) < c:
                deps[k] = c
        waits = []
        kn = self.know[stream]
        for F, c in sorted(deps.items()):
            if kn.get(F, 0) >= c:
                continue
            sn = self.snap[F][c // STEP[F] - 1]
            for kk, vv in sn.items():
                if kn.get(kk, 0) < vv:
                    kn[kk] = vv
            if kn.get(F, 0) < c:
                kn[F] = c
            waits.append((F, c))
            self.nwaits += 1
        self.cnt[E] += STEP[E]
        c = self.cnt[E]
        sn = dict(kn)
        sn[E] = c
        self.snap[E].append(sn)
        if stream == 'T' == E:
            kn['T'] = c
        self.prog[stream].append((fname, args, kwargs, waits, E))
        for w in writes:
            self.lastw[w] = (E, c)
            self.readers[w] = {}
        for r in reads:
            self.readers[r][E] = c
        return c

    def barrier(self):
        for s in STREAMS:
            kn = self.know[s]
            waits = []
            for F in SEMKEYS:
                c = self.cnt[F]
                if c and kn.get(F, 0) < c:
                    sn = self.snap[F][c // STEP[F] - 1]
                    for kk, vv in sn.items():
                        if kn.get(kk, 0) < vv:
                            kn[kk] = vv
                    kn[F] = c
                    waits.append((F, c))
            if waits:
                self.prog[s].append((None, (), {}, waits, None))

    def dma(self, q, out, in_, reads=(), writes=(), **kw):
        idx = self.qrr[q]
        self.qrr[q] = (idx + 1) % NQROT
        E = f"Q{q}{idx}"
        ed = [(E, self.cnt[E])] if self.cnt[E] else []
        return self.emit(E, 'dma_start', reads=reads, writes=writes,
                         extra_deps=ed, out=out, in_=in_, **kw)

    def replay(self, block):
        holders = {}

        def run(stream):
            def body(eng):
                holders[stream] = eng
                for fname, args, kwargs, waits, E in self.prog[stream]:
                    for F, c in waits:
                        eng.wait_ge(self.sems[F], c)
                    if fname is None:
                        continue
                    inst = getattr(eng, fname)(*args, **kwargs)
                    inst.then_inc(self.sems[E], STEP[E])
            return body

        block.tensor(run('T'))
        block.vector(run('V'))
        block.scalar(run('S'))
        block.gpsimd(run('P'))
        block.sync(run('Y'))


def build_nc(cfg=None):
    cfg = dict(cfg or {})
    p1_cols = cfg.get('p1_cols', 8)
    tiles = cfg.get('tiles', [0, 1])
    heads = cfg.get('heads', H)
    experts = cfg.get('experts', NE)
    do_attn = cfg.get('do_attn', True)
    do_moe = cfg.get('do_moe', True)
    do_shared = cfg.get('do_shared', True)
    do_p1 = cfg.get('do_p1', True)

    nc = bass.Bass("TRN2", target_bir_lowering=False)

    dr = {}
    def din(name, shape, dt=F32):
        dr[name] = nc.dram_tensor(name, shape, dt, kind="ExternalInput").ap()
        return dr[name]

    xseq = din("xseq", [SEQ, D])
    xown = din("xown", [2048, D])
    wq_t = din("wq_t", [D, D])
    wk_t = din("wk_t", [D, D])
    wv_t = din("wv_t", [D, D])
    wo_t = din("wo_t", [D, D])
    gw_t = din("gw_t", [NE, D, FM])
    uw_t = din("uw_t", [NE, D, FM])
    dw_t = din("dw_t", [NE, FM, D])
    sg_t = din("sg_t", [D, FSH])
    su_t = din("su_t", [D, FSH])
    sd_t = din("sd_t", [FSH, D])
    rt = din("rt", [D, NE])
    ebi = din("eb", [128, NE])
    ident_in = din("ident", [128, 128])
    masks = din("masks", [NSLOT, 128, 512])

    out_d = nc.dram_tensor("out", [DC, 128, 2048], F32, kind="ExternalOutput").ap()
    kscr = nc.dram_tensor("kscr", [H, DH, SEQ], BF16, kind="Internal").ap()
    vscr = nc.dram_tensor("vscr", [SEQ, D], BF16, kind="Internal").ap()

    with contextlib.ExitStack() as top:
        B = Bld(nc, top)
        sb = lambda n, shp, dt, st=top, side=None: st.enter_context(nc.sbuf_tensor(n, shp, dt, side=side))
        pa = [top.enter_context(nc.psum_tensor(f"pa{i}", [128, 512], F32))
              for i in range(6)]
        pt = [top.enter_context(nc.psum_tensor(f"pt{i}", [128, 1024], BF16))
              for i in range(2)]

        idt = sb("idt", [128, 128], BF16)
        idtf = sb("idtf", [128, 128], F32)
        ebbc = sb("ebbc", [128, NE], F32)
        B.dma('P', idt[:], ident_in, writes=['idt'])
        B.dma('Y', idtf[:], ident_in, writes=['idtf'])
        B.dma('Y', ebbc[:], ebi, writes=['ebbc'])

        def emit_ln(stx, par, x_rows_ap, dst, dst_name, dst_toff, bufs):
            xt = bufs['xt'][par]; xtn = f"xt{par}"
            sq = bufs['sq'][par]; sqn = f"sq{par}"
            xn = bufs['xn'][par]; xnn = f"xn{par}"
            stt_ = bufs['st'][par]; stn = f"lst{par}"
            B.dma('Y', xt[:], x_rows_ap, writes=[xtn])
            B.emit('V', 'reduce_sum', stt_[:, 0:1], xt[:], AX.X,
                   reads=[xtn], writes=[stn])
            B.emit('S', 'activation', sq[:], xt[:], AF.Square,
                   accum_out=stt_[:, 2:3], reads=[xtn], writes=[sqn, stn])
            B.emit('V', 'tensor_scalar', stt_[:, 1:2], stt_[:, 0:1],
                   -1.0 / D, None, ALU.mult, reads=[stn], writes=[stn])
            B.emit('V', 'tensor_scalar', stt_[:, 3:4], stt_[:, 2:3],
                   1.0 / D, None, ALU.mult, reads=[stn], writes=[stn])
            B.emit('V', 'tensor_tensor', stt_[:, 4:5], stt_[:, 1:2], stt_[:, 1:2],
                   ALU.mult, reads=[stn], writes=[stn])
            B.emit('V', 'tensor_tensor', stt_[:, 5:6], stt_[:, 3:4], stt_[:, 4:5],
                   ALU.subtract, reads=[stn], writes=[stn])
            B.emit('V', 'tensor_scalar', stt_[:, 5:6], stt_[:, 5:6], 1e-5, None,
                   ALU.add, reads=[stn], writes=[stn])
            B.emit('S', 'activation', stt_[:, 6:7], stt_[:, 5:6], AF.Sqrt,
                   reads=[stn], writes=[stn])
            B.emit('V', 'reciprocal', stt_[:, 7:8], stt_[:, 6:7],
                   reads=[stn], writes=[stn])
            B.emit('V', 'tensor_scalar', xn[:], xt[:], stt_[:, 1:2], stt_[:, 7:8],
                   ALU.add, ALU.mult, reads=[xtn, stn], writes=[xnn])
            for g8 in range(2):
                ptb = pt[g8]
                ptn = f"pt{g8}"
                for k in range(8):
                    dc = g8 * 8 + k
                    B.emit('T', 'transpose', ptb[:, k * 128:(k + 1) * 128],
                           xn[:, dc * 128:(dc + 1) * 128], idt[:],
                           reads=[xnn, 'idt'], writes=[ptn])
                B.emit('V', 'tensor_copy',
                       dst[:, g8 * 8:(g8 + 1) * 8, dst_toff:dst_toff + 128],
                       ptb[:].rearrange("p (a b) -> p a b", a=8),
                       reads=[ptn], writes=[dst_name])

        if do_p1:
            with contextlib.ExitStack() as s1:
                wk = sb("wk", [128, DC, D], BF16, s1)
                wv = sb("wv", [128, DC, D], BF16, s1)
                xnf = sb("xnf", [128, DC, 512], BF16, s1)
                kcol = sb("kcol", [128, H, 512], BF16, s1)
                bufs = {
                    'xt': [sb(f"p1xt{i}", [128, D], F32, s1) for i in range(2)],
                    'sq': [sb(f"p1sq{i}", [128, D], BF16, s1) for i in range(2)],
                    'xn': [sb(f"p1xn{i}", [128, D], BF16, s1) for i in range(2)],
                    'st': [sb(f"p1st{i}", [128, 8], F32, s1) for i in range(2)],
                }
                vst = [sb(f"vst{i}", [128, D], BF16, s1) for i in range(2)]
                B.dma('P', wk[:], wk_t.rearrange("(c p) o -> p c o", p=128),
                      writes=['wk'])
                B.dma('P', wv[:], wv_t.rearrange("(c p) o -> p c o", p=128),
                      writes=['wv'])
                for col in range(p1_cols):
                    for tt in range(4):
                        r0 = col * 512 + tt * 128
                        emit_ln(s1, tt % 2, xseq[r0:r0 + 128, :],
                                xnf, 'xnf', tt * 128, bufs)
                    for h in range(H):
                        bank = pa[h % 2]
                        bname = f"pa{h % 2}"
                        for dc in range(DC):
                            B.emit('T', 'matmul', bank[:],
                                   wk[:, dc, h * 128:(h + 1) * 128],
                                   xnf[:, dc, :],
                                   start=(dc == 0), stop=(dc == DC - 1),
                                   reads=['wk', 'xnf'], writes=[bname])
                        B.emit('S', 'mul', kcol[:, h, :], bank[:], KSCALE,
                               reads=[bname], writes=['kcol'])
                    B.dma('Y',
                          kscr[:, :, col * 512:(col + 1) * 512]
                          .rearrange("h p k -> p h k"),
                          kcol[:], reads=['kcol'], writes=['kscr'])
                    for tt in range(4):
                        vpar = tt % 2
                        vb = vst[vpar]
                        vbn = f"vst{vpar}"
                        for ec in range(4):
                            bank = pa[2 + ec % 2]
                            bname = f"pa{2 + ec % 2}"
                            for dc in range(DC):
                                B.emit('T', 'matmul', bank[:],
                                       xnf[:, dc, tt * 128:(tt + 1) * 128],
                                       wv[:, dc, ec * 512:(ec + 1) * 512],
                                       start=(dc == 0), stop=(dc == DC - 1),
                                       reads=['xnf', 'wv'], writes=[bname])
                            B.emit('V', 'tensor_copy',
                                   vb[:, ec * 512:(ec + 1) * 512], bank[:],
                                   reads=[bname], writes=[vbn])
                        r0 = col * 512 + tt * 128
                        B.dma('Y', vscr[r0:r0 + 128, :], vb[:],
                              reads=[vbn], writes=['vscr'])
                B.barrier()

        acc_live = False
        for tau in tiles:
            with contextlib.ExitStack() as s2:
                xnt = sb(f"xnt{tau}", [128, DC, TILE], BF16, s2)
                xntn = f"xnt{tau}"
                wall = sb(f"wall{tau}", [128, 8, NE], F32, s2)
                walln = f"wall{tau}"
                with contextlib.ExitStack() as sl:
                    bufs = {
                        'xt': [sb(f"t{tau}xt{i}", [128, D], F32, sl) for i in range(2)],
                        'sq': [sb(f"t{tau}sq{i}", [128, D], BF16, sl) for i in range(2)],
                        'xn': [sb(f"t{tau}xn{i}", [128, D], BF16, sl) for i in range(2)],
                        'st': [sb(f"t{tau}st{i}", [128, 8], F32, sl) for i in range(2)],
                    }
                    for ts in range(8):
                        r0 = tau * 1024 + ts * 128
                        emit_ln(sl, ts % 2, xown[r0:r0 + 128, :],
                                xnt, xntn, ts * 128, bufs)
                    B.barrier()

                with contextlib.ExitStack() as sr:
                    rtb = sb(f"rtb{tau}", [128, DC, NE], BF16, sr)
                    lg = sb(f"lg{tau}", [128, NE], F32, sr)
                    sc = sb(f"sc{tau}", [128, NE], F32, sr)
                    sfc = sb(f"sfc{tau}", [128, NE], F32, sr)
                    grp6 = sb(f"grp6{tau}", [128, 4, 6], F32, sr)
                    gs = sb(f"gs{tau}", [128, 4], F32, sr)
                    gsc = sb(f"gsc{tau}", [128, 12], F32, sr)
                    is1 = sb(f"is1{tau}", [128, 4], F32, sr)
                    is2 = sb(f"is2{tau}", [128, 4], F32, sr)
                    gm = sb(f"gm{tau}", [128, 4], F32, sr)
                    mk = sb(f"mk{tau}", [128, NE], F32, sr)
                    selm = sb(f"selm{tau}", [128, NE], F32, sr)
                    isr = sb(f"isr{tau}", [128, NE], F32, sr)
                    wraw = sb(f"wraw{tau}", [128, NE], F32, sr)
                    B.dma('P', rtb[:], rt.rearrange("(c p) n -> p c n", p=128),
                          writes=['rtb'])
                    for ts in range(8):
                        bank = pa[0]
                        for dc in range(DC):
                            B.emit('T', 'matmul', bank[:, 0:NE],
                                   xnt[:, dc, ts * 128:(ts + 1) * 128],
                                   rtb[:, dc, :],
                                   start=(dc == 0), stop=(dc == DC - 1),
                                   reads=[xntn, 'rtb'], writes=['pa0'])
                        B.emit('V', 'tensor_copy', lg[:], bank[:, 0:NE],
                               reads=['pa0'], writes=['lg'])
                        B.emit('S', 'activation', sc[:], lg[:], AF.Sigmoid,
                               reads=['lg'], writes=['sc'])
                        B.emit('V', 'tensor_tensor', sfc[:], sc[:], ebbc[:],
                               ALU.add, reads=['sc', 'ebbc'], writes=['sfc'])
                        pairs = [(0, 1), (0, 2), (0, 3), (1, 2), (1, 3), (2, 3)]
                        for pi, (a, b) in enumerate(pairs):
                            B.emit('V', 'tensor_tensor', grp6[:, :, pi],
                                   sfc[:].rearrange("p (g e) -> p g e", e=4)[:, :, a],
                                   sfc[:].rearrange("p (g e) -> p g e", e=4)[:, :, b],
                                   ALU.add, reads=['sfc'], writes=['grp6'])
                        B.emit('V', 'reduce_max', gs[:].rearrange("p (g o) -> p g o", o=1),
                               grp6[:], AX.X, reads=['grp6'], writes=['gs'])
                        B.emit('V', 'reduce_max', gsc[:, 0:1], gs[:], AX.X,
                               reads=['gs'], writes=['gsc'])
                        B.emit('V', 'tensor_scalar', is1[:], gs[:], gsc[:, 0:1],
                               None, ALU.is_ge, reads=['gs', 'gsc'], writes=['is1'])
                        B.emit('V', 'scalar_tensor_tensor', gm[:], is1[:], NEG,
                               gs[:], ALU.mult, ALU.add,
                               reads=['is1', 'gs'], writes=['gm'])
                        B.emit('V', 'reduce_max', gsc[:, 1:2], gm[:], AX.X,
                               reads=['gm'], writes=['gsc'])
                        B.emit('V', 'tensor_scalar', is2[:], gm[:], gsc[:, 1:2],
                               None, ALU.is_ge, reads=['gm', 'gsc'], writes=['is2'])
                        B.emit('V', 'tensor_tensor', gm[:], is1[:], is2[:],
                               ALU.add, reads=['is1', 'is2'], writes=['gm'])
                        B.emit('V', 'tensor_scalar', gm[:], gm[:], -1.0, -NEG,
                               ALU.add, ALU.mult, reads=['gm'], writes=['gm'])
                        for g in range(4):
                            B.emit('V', 'tensor_scalar', mk[:, 4 * g:4 * g + 4],
                                   sfc[:, 4 * g:4 * g + 4], gm[:, g:g + 1], None,
                                   ALU.add, reads=['sfc', 'gm'], writes=['mk'])
                        B.emit('V', 'memset', selm[:], 0.0, writes=['selm'])
                        for r in range(4):
                            B.emit('V', 'reduce_max', gsc[:, 2:3], mk[:], AX.X,
                                   reads=['mk'], writes=['gsc'])
                            B.emit('V', 'tensor_scalar', isr[:], mk[:],
                                   gsc[:, 2:3], None, ALU.is_ge,
                                   reads=['mk', 'gsc'], writes=['isr'])
                            B.emit('V', 'tensor_tensor', selm[:], selm[:], isr[:],
                                   ALU.max, reads=['selm', 'isr'], writes=['selm'])
                            if r < 3:
                                B.emit('V', 'scalar_tensor_tensor', mk[:], isr[:],
                                       NEG, mk[:], ALU.mult, ALU.add,
                                       reads=['isr', 'mk'], writes=['mk'])
                        B.emit('V', 'tensor_tensor', wraw[:], sc[:], selm[:],
                               ALU.mult, reads=['sc', 'selm'], writes=['wraw'])
                        B.emit('V', 'reduce_sum', gsc[:, 3:4], wraw[:], AX.X,
                               reads=['wraw'], writes=['gsc'])
                        B.emit('V', 'tensor_scalar', gsc[:, 4:5], gsc[:, 3:4],
                               1e-20, None, ALU.add, reads=['gsc'], writes=['gsc'])
                        B.emit('V', 'reciprocal', gsc[:, 5:6], gsc[:, 4:5],
                               reads=['gsc'], writes=['gsc'])
                        B.emit('V', 'tensor_scalar', wall[:, ts, :], wraw[:],
                               gsc[:, 5:6], 2.5, ALU.mult, ALU.mult,
                               reads=['wraw', 'gsc'], writes=[walln])
                    B.barrier()

                s2b = s2.enter_context(contextlib.ExitStack())
                oall = sb(f"oall{tau}", [128, H, TILE], BF16, s2b)
                oalln = f"oall{tau}"
                if do_attn:
                    B.emit('V', 'memset', oall[:], 0.0, writes=[oalln])
                    with contextlib.ExitStack() as sa:
                        qall = sb(f"qall{tau}", [128, H, TILE], BF16, sa)
                        kh = [sb(f"kh{tau}{i}", [128, SEQ], BF16, sa) for i in range(2)]
                        vh = [sb(f"vh{tau}{i}", [128, 32, 128], BF16, sa) for i in range(2)]
                        bsb = sb(f"bsb{tau}", [128, SEQ], F32, sa)
                        scsb = sb(f"scsb{tau}", [128, SEQ], F32, sa)
                        prb = sb(f"prb{tau}", [128, SEQ], BF16, sa)
                        ptb = sb(f"ptbuf{tau}", [128, 32, 256], BF16, sa)
                        sums = sb(f"sums{tau}", [128, 8], F32, sa)
                        astat = sb(f"astat{tau}", [128, 4], F32, sa)
                        wqb = [sb(f"wq{tau}{i}", [128, DC, 128], BF16, sa)
                               for i in range(2)]
                        mskt = sb(f"mskt{tau}", [128, 8, 512], BF16, sa)
                        B.dma('P', mskt[:],
                              masks[tau * 8:(tau + 1) * 8]
                              .rearrange("s p k -> p s k"),
                              writes=['mskt'])
                        kwt = 512 * (4 * tau + 4)
                        slope = slopes()

                        for h in range(heads):
                            wpar = h % 2
                            if h == 0:
                                B.dma('P', wqb[0][:],
                                      wq_t[:, 0:128].rearrange("(c p) o -> p c o", p=128),
                                      writes=['wq0'])
                            if h + 1 < heads:
                                B.dma('P', wqb[(h + 1) % 2][:],
                                      wq_t[:, (h + 1) * 128:(h + 2) * 128]
                                      .rearrange("(c p) o -> p c o", p=128),
                                      writes=[f"wq{(h + 1) % 2}"])
                            for qc in range(2):
                                bank = pa[qc]
                                for dc in range(DC):
                                    B.emit('T', 'matmul', bank[:],
                                           wqb[wpar][:, dc, :],
                                           xnt[:, dc, qc * 512:(qc + 1) * 512],
                                           start=(dc == 0), stop=(dc == DC - 1),
                                           reads=[f"wq{wpar}", xntn],
                                           writes=[f"pa{qc}"])
                                B.emit('S', 'copy',
                                       qall[:, h, qc * 512:(qc + 1) * 512], bank[:],
                                       reads=[f"pa{qc}"], writes=['qall'])

                        for h in range(heads):
                            hpar = h % 2
                            khn, vhn = f"kh{hpar}", f"vh{hpar}"
                            if h == 0:
                                B.dma('Y', kh[0][:, 0:kwt], kscr[0, :, 0:kwt],
                                      reads=['kscr'], writes=['kh0'])
                                B.dma('Y', vh[0][:, 0:kwt // 128, :],
                                      vscr[0:kwt, 0:128]
                                      .rearrange("(a p) e -> p a e", p=128),
                                      reads=['vscr'], writes=['vh0'])
                            if h + 1 < heads:
                                np_ = (h + 1) % 2
                                B.dma('Y', kh[np_][:, 0:kwt],
                                      kscr[h + 1, :, 0:kwt],
                                      reads=['kscr'], writes=[f"kh{np_}"])
                                B.dma('Y', vh[np_][:, 0:kwt // 128, :],
                                      vscr[0:kwt, (h + 1) * 128:(h + 2) * 128]
                                      .rearrange("(a p) e -> p a e", p=128),
                                      reads=['vscr'], writes=[f"vh{np_}"])
                            B.emit('P', 'iota', bsb[:, 0:kwt].bitcast(I32),
                                   [[1, kwt]], channel_multiplier=0,
                                   writes=['bsb'])
                            B.emit('V', 'tensor_single_scalar', bsb[:, 0:kwt],
                                   bsb[:, 0:kwt].bitcast(I32), slope[h], ALU.mult,
                                   reads=['bsb'], writes=['bsb'])
                            for pr in range(4):
                                cls = 4 * tau + pr + 1
                                kw = 512 * cls
                                nkt = kw // 128
                                for m in range(2):
                                    j = 2 * pr + m
                                    for ch in range(cls):
                                        bank = pa[ch % 2]
                                        bname = f"pa{ch % 2}"
                                        B.emit('T', 'matmul', bank[:],
                                               qall[:, h, j * 128:(j + 1) * 128],
                                               kh[hpar][:, ch * 512:(ch + 1) * 512],
                                               start=True, stop=True,
                                               reads=['qall', khn], writes=[bname])
                                        B.emit('V', 'tensor_tensor',
                                               scsb[:, ch * 512:(ch + 1) * 512],
                                               bank[:],
                                               bsb[:, ch * 512:(ch + 1) * 512],
                                               ALU.add,
                                               reads=[bname, 'bsb'], writes=['scsb'])
                                    B.emit('V', 'tensor_tensor',
                                           scsb[:, kw - 512:kw],
                                           scsb[:, kw - 512:kw], mskt[:, j, :],
                                           ALU.add,
                                           reads=['scsb', 'mskt'],
                                           writes=['scsb'])
                                    B.emit('V', 'reduce_max', astat[:, 0:1],
                                           scsb[:, 0:kw], AX.X, negate=True,
                                           reads=['scsb'], writes=['astat'])
                                    for ch in range(cls):
                                        B.emit('S', 'activation',
                                               prb[:, ch * 512:(ch + 1) * 512],
                                               scsb[:, ch * 512:(ch + 1) * 512],
                                               AF.Exp, bias=astat[:, 0:1],
                                               accum_out=sums[:, ch:ch + 1],
                                               reads=['scsb', 'astat'],
                                               writes=['prb', 'sums'])
                                    B.emit('V', 'reduce_sum', astat[:, 1:2],
                                           sums[:, 0:cls], AX.X,
                                           reads=['sums'], writes=['astat'])
                                    B.emit('V', 'reciprocal', astat[:, 2:3],
                                           astat[:, 1:2],
                                           reads=['astat'], writes=['astat'])
                                    B.emit('V', 'tensor_scalar', prb[:, 0:kw],
                                           prb[:, 0:kw], astat[:, 2:3], None,
                                           ALU.mult,
                                           reads=['prb', 'astat'], writes=['prb'])
                                    for g8 in range((nkt + 7) // 8):
                                        gn = min(8, nkt - g8 * 8)
                                        bank = pt[g8 % 2]
                                        bname = f"pt{g8 % 2}"
                                        for k in range(gn):
                                            kt = g8 * 8 + k
                                            B.emit('T', 'transpose',
                                                   bank[:, k * 128:(k + 1) * 128],
                                                   prb[:, kt * 128:(kt + 1) * 128],
                                                   idt[:],
                                                   reads=['prb', 'idt'],
                                                   writes=[bname])
                                        B.emit('S', 'copy',
                                               ptb[:, g8 * 8:g8 * 8 + gn,
                                                   m * 128:(m + 1) * 128],
                                               bank[:, 0:gn * 128].rearrange(
                                                   "p (a b) -> p a b", b=128),
                                               reads=[bname], writes=['ptbuf'])
                                for kt in range(nkt):
                                    B.emit('T', 'matmul', pa[2][:, 0:256],
                                           vh[hpar][:, kt, :], ptb[:, kt, :],
                                           start=(kt == 0), stop=(kt == nkt - 1),
                                           reads=[vhn, 'ptbuf'], writes=['pa2'])
                                B.emit('S', 'copy',
                                       oall[:, h, pr * 256:(pr + 1) * 256],
                                       pa[2][:, 0:256],
                                       reads=['pa2'], writes=[oalln])
                        B.barrier()

                acc = sb(f"acc{tau}", [128, DC, TILE], F32, s2, side="right")
                accn = f"acc{tau}"
                with contextlib.ExitStack() as sw:
                    xtw = [sb(f"w{tau}xt{i}", [128, D], F32, sw) for i in range(2)]
                    wob = [sb(f"wo{tau}{i}", [128, DC, 128], BF16, sw)
                           for i in range(2)]
                    for ts in range(8):
                        par = ts % 2
                        r0 = tau * 1024 + ts * 128
                        B.dma('Y', xtw[par][:], xown[r0:r0 + 128, :],
                              writes=[f"wxt{par}"])
                        for g4 in range(4):
                            bank = pa[3 + g4 % 3]
                            bname = f"pa{3 + g4 % 3}"
                            for k in range(4):
                                dc = g4 * 4 + k
                                B.emit('T', 'transpose',
                                       bank[:, k * 128:(k + 1) * 128],
                                       xtw[par][:, dc * 128:(dc + 1) * 128],
                                       idtf[:],
                                       reads=[f"wxt{par}", 'idtf'], writes=[bname])
                            B.emit('V', 'tensor_copy',
                                   acc[:, g4 * 4:(g4 + 1) * 4, ts * 128:(ts + 1) * 128],
                                   bank[:].rearrange("p (a b) -> p a b", a=4),
                                   reads=[bname], writes=[accn])
                    if do_attn:
                        for oc in range(DC):
                            wpar = oc % 2
                            if oc == 0:
                                B.dma('P', wob[0][:],
                                      wo_t[:, 0:128].rearrange("(c p) o -> p c o", p=128),
                                      writes=['wo0'])
                            if oc + 1 < DC:
                                B.dma('P', wob[(oc + 1) % 2][:],
                                      wo_t[:, (oc + 1) * 128:(oc + 2) * 128]
                                      .rearrange("(c p) o -> p c o", p=128),
                                      writes=[f"wo{(oc + 1) % 2}"])
                            for qc in range(2):
                                bank = pa[qc]
                                bname = f"pa{qc}"
                                for ec in range(H):
                                    B.emit('T', 'matmul', bank[:],
                                           wob[wpar][:, ec, :],
                                           oall[:, ec, qc * 512:(qc + 1) * 512],
                                           start=(ec == 0), stop=(ec == H - 1),
                                           reads=[f"wo{wpar}", oalln],
                                           writes=[bname])
                                B.emit('V', 'tensor_tensor',
                                       acc[:, oc, qc * 512:(qc + 1) * 512],
                                       bank[:],
                                       acc[:, oc, qc * 512:(qc + 1) * 512],
                                       ALU.add,
                                       reads=[bname, accn], writes=[accn])
                    B.barrier()

                s2b.close()

                if do_moe:
                    with contextlib.ExitStack() as sm:
                        wgb = [sb(f"wg{tau}{i}", [128, DC, 512], BF16, sm)
                               for i in range(2)]
                        wub = [sb(f"wu{tau}{i}", [128, DC, 512], BF16, sm)
                               for i in range(2)]
                        dwb = [sb(f"dw{tau}{i}", [128, 8, 128], BF16, sm)
                               for i in range(2)]
                        act = sb(f"act{tau}", [128, 8, TILE], BF16, sm)
                        sg_ = [sb(f"sg_{tau}{i}", [128, 512], BF16, sm)
                               for i in range(2)]
                        ast = [sb(f"ast{tau}{i}", [128, 512], BF16, sm)
                               for i in range(2)]

                        def gu_dma(n, fc, par):
                            B.dma('P', wgb[par][:],
                                  gw_t[n, :, fc * 512:(fc + 1) * 512]
                                  .rearrange("(c p) f -> p c f", p=128),
                                  writes=[f"wg{par}"])
                            B.dma('P', wub[par][:],
                                  uw_t[n, :, fc * 512:(fc + 1) * 512]
                                  .rearrange("(c p) f -> p c f", p=128),
                                  writes=[f"wu{par}"])

                        def dw_dma(n, hc, par):
                            B.dma('P', dwb[par][:],
                                  dw_t[n, :, hc * 128:(hc + 1) * 128]
                                  .rearrange("(c p) h2 -> p c h2", p=128),
                                  writes=[f"dw{par}"])

                        gu_dma(0, 0, 0)
                        for ni in range(experts):
                            n = ni
                            for fc in range(2):
                                gpar = fc % 2
                                if fc + 1 < 2:
                                    gu_dma(n, fc + 1, (fc + 1) % 2)
                                elif ni + 1 < experts:
                                    gu_dma(ni + 1, 0, 0)
                                for ts in range(8):
                                    pg, pu = pa[ts % 2], pa[2 + ts % 2]
                                    pgn, pun = f"pa{ts % 2}", f"pa{2 + ts % 2}"
                                    for dc in range(DC):
                                        B.emit('T', 'matmul', pg[:],
                                               xnt[:, dc, ts * 128:(ts + 1) * 128],
                                               wgb[gpar][:, dc, :],
                                               start=(dc == 0), stop=(dc == DC - 1),
                                               reads=[xntn, f"wg{gpar}"],
                                               writes=[pgn])
                                        B.emit('T', 'matmul', pu[:],
                                               xnt[:, dc, ts * 128:(ts + 1) * 128],
                                               wub[gpar][:, dc, :],
                                               start=(dc == 0), stop=(dc == DC - 1),
                                               reads=[xntn, f"wu{gpar}"],
                                               writes=[pun])
                                    spar = ts % 2
                                    B.emit('S', 'activation', sg_[spar][:], pg[:],
                                           AF.Silu, reads=[pgn],
                                           writes=[f"sg_{spar}"])
                                    B.emit('V', 'scalar_tensor_tensor',
                                           ast[spar][:], pu[:],
                                           wall[:, ts, n:n + 1], sg_[spar][:],
                                           ALU.mult, ALU.mult,
                                           reads=[pun, walln, f"sg_{spar}"],
                                           writes=[f"ast{spar}"])
                                    bank = pt[ts % 2]
                                    bname = f"pt{ts % 2}"
                                    for ft in range(4):
                                        B.emit('T', 'transpose',
                                               bank[:, ft * 128:(ft + 1) * 128],
                                               ast[spar][:, ft * 128:(ft + 1) * 128],
                                               idt[:],
                                               reads=[f"ast{spar}", 'idt'],
                                               writes=[bname])
                                    B.emit('V', 'tensor_copy',
                                           act[:, fc * 4:fc * 4 + 4,
                                               ts * 128:(ts + 1) * 128],
                                           bank[:, 0:512].rearrange(
                                               "p (a b) -> p a b", a=4),
                                           reads=[bname], writes=['act'])
                            dw_dma(n, 0, 0)
                            for hc in range(DC):
                                dpar = hc % 2
                                if hc + 1 < DC:
                                    dw_dma(n, hc + 1, (hc + 1) % 2)
                                elif ni + 1 < experts:
                                    pass
                                for tc in range(2):
                                    bank = pa[4 + tc]
                                    bname = f"pa{4 + tc}"
                                    for fcc in range(8):
                                        B.emit('T', 'matmul', bank[:],
                                               dwb[dpar][:, fcc, :],
                                               act[:, fcc, tc * 512:(tc + 1) * 512],
                                               start=(fcc == 0), stop=(fcc == 7),
                                               reads=[f"dw{dpar}", 'act'],
                                               writes=[bname])
                                    B.emit('V', 'tensor_tensor',
                                           acc[:, hc, tc * 512:(tc + 1) * 512],
                                           bank[:],
                                           acc[:, hc, tc * 512:(tc + 1) * 512],
                                           ALU.add,
                                           reads=[bname, accn], writes=[accn])
                        B.barrier()

                if do_shared:
                    with contextlib.ExitStack() as ss:
                        acts = sb(f"acts{tau}", [128, 16, TILE], BF16, ss)
                        sgb = [sb(f"sgb{tau}{i}", [128, DC, 128], BF16, ss)
                               for i in range(2)]
                        sub = [sb(f"sub{tau}{i}", [128, DC, 128], BF16, ss)
                               for i in range(2)]
                        sdb = [sb(f"sdb{tau}{i}", [128, 16, 128], BF16, ss)
                               for i in range(2)]
                        sgs = [sb(f"sgs{tau}{i}", [128, 512], BF16, ss)
                               for i in range(2)]

                        def sh_dma(fc, par):
                            B.dma('P', sgb[par][:],
                                  sg_t[:, fc * 128:(fc + 1) * 128]
                                  .rearrange("(c p) f -> p c f", p=128),
                                  writes=[f"sgb{par}"])
                            B.dma('P', sub[par][:],
                                  su_t[:, fc * 128:(fc + 1) * 128]
                                  .rearrange("(c p) f -> p c f", p=128),
                                  writes=[f"sub{par}"])

                        def sd_dma(hc, par):
                            B.dma('P', sdb[par][:],
                                  sd_t[:, hc * 128:(hc + 1) * 128]
                                  .rearrange("(c p) h2 -> p c h2", p=128),
                                  writes=[f"sdb{par}"])

                        sh_dma(0, 0)
                        for fc in range(16):
                            par = fc % 2
                            if fc + 1 < 16:
                                sh_dma(fc + 1, (fc + 1) % 2)
                            for tc in range(2):
                                pg, pu = pa[tc], pa[2 + tc]
                                pgn, pun = f"pa{tc}", f"pa{2 + tc}"
                                for dc in range(DC):
                                    B.emit('T', 'matmul', pg[:],
                                           sgb[par][:, dc, :],
                                           xnt[:, dc, tc * 512:(tc + 1) * 512],
                                           start=(dc == 0), stop=(dc == DC - 1),
                                           reads=[f"sgb{par}", xntn], writes=[pgn])
                                    B.emit('T', 'matmul', pu[:],
                                           sub[par][:, dc, :],
                                           xnt[:, dc, tc * 512:(tc + 1) * 512],
                                           start=(dc == 0), stop=(dc == DC - 1),
                                           reads=[f"sub{par}", xntn], writes=[pun])
                                B.emit('S', 'activation', sgs[tc][:], pg[:],
                                       AF.Silu, reads=[pgn], writes=[f"sgs{tc}"])
                                B.emit('V', 'scalar_tensor_tensor',
                                       acts[:, fc, tc * 512:(tc + 1) * 512],
                                       pu[:], 1.0, sgs[tc][:],
                                       ALU.mult, ALU.mult,
                                       reads=[pun, f"sgs{tc}"], writes=['acts'])
                        sd_dma(0, 0)
                        for hc in range(DC):
                            par = hc % 2
                            if hc + 1 < DC:
                                sd_dma(hc + 1, (hc + 1) % 2)
                            for tc in range(2):
                                bank = pa[4 + tc]
                                bname = f"pa{4 + tc}"
                                for fcc in range(16):
                                    B.emit('T', 'matmul', bank[:],
                                           sdb[par][:, fcc, :],
                                           acts[:, fcc, tc * 512:(tc + 1) * 512],
                                           start=(fcc == 0), stop=(fcc == 15),
                                           reads=[f"sdb{par}", 'acts'],
                                           writes=[bname])
                                B.emit('V', 'tensor_tensor',
                                       acc[:, hc, tc * 512:(tc + 1) * 512],
                                       bank[:],
                                       acc[:, hc, tc * 512:(tc + 1) * 512],
                                       ALU.add,
                                       reads=[bname, accn], writes=[accn])
                        B.barrier()

                B.dma('Y',
                      out_d[:, :, tau * 1024:(tau + 1) * 1024]
                      .rearrange("c p t -> p c t"),
                      acc[:], reads=[accn], writes=['out'])
                B.barrier()

        fin = [(k, B.cnt[k]) for k in SEMKEYS
               if k.startswith('Q') and B.cnt[k]]
        B.prog['Y'].append((None, (), {}, fin, None))

        with nc.Block() as block:
            B.replay(block)

    return nc



def _prep_core_inputs(core, x, wq, wk, wv, wo, gwt, uwt, dwt, sgt, sut, sdt,
                      rtw, ebv, idn, maskcache={}):
    b = core >> 1
    half = core & 1
    xb = np.ascontiguousarray(x[b])
    rows = []
    for j in range(NSLOT):
        g = slot_global(j, half)
        rows.append(xb[128 * g:128 * (g + 1)])
    xo = np.ascontiguousarray(np.concatenate(rows, axis=0))
    if half not in maskcache:
        mk = np.zeros((NSLOT, 128, 512), dtype=np.float32)
        for j in range(NSLOT):
            g = slot_global(j, half)
            cls = j // 2 + 1
            q = 128 * g + np.arange(128)[:, None]
            kk = 512 * (cls - 1) + np.arange(512)[None, :]
            mk[j] = np.where(kk <= q, 0.0, NEG).astype(np.float32)
        maskcache[half] = mk
    return {
        "xseq": xb, "xown": xo,
        "wq_t": wq, "wk_t": wk, "wv_t": wv, "wo_t": wo,
        "gw_t": gwt, "uw_t": uwt, "dw_t": dwt,
        "sg_t": sgt, "su_t": sut, "sd_t": sdt,
        "rt": rtw, "eb": ebv, "ident": idn, "masks": maskcache[half],
    }


def kernel(L0_attn_x, L0_attn_Wq, L0_attn_Wk, L0_attn_Wv, L0_attn_Wo,
           L0_ffn_gate_weight, L0_ffn_e_score_correction_bias,
           L0_ffn_gate_w, L0_ffn_up_w, L0_ffn_down_w,
           L0_ffn_sh_gate_w, L0_ffn_sh_up_w, L0_ffn_sh_down_w):
    x = np.asarray(L0_attn_x, dtype=np.float32)
    T = lambda a: np.ascontiguousarray(np.asarray(a, np.float32).T)
    wq, wk = T(L0_attn_Wq), T(L0_attn_Wk)
    wv, wo = T(L0_attn_Wv), T(L0_attn_Wo)
    gwt = np.ascontiguousarray(np.asarray(L0_ffn_gate_w, np.float32).transpose(0, 2, 1))
    uwt = np.ascontiguousarray(np.asarray(L0_ffn_up_w, np.float32).transpose(0, 2, 1))
    dwt = np.ascontiguousarray(np.asarray(L0_ffn_down_w, np.float32).transpose(0, 2, 1))
    sgt, sut = T(L0_ffn_sh_gate_w), T(L0_ffn_sh_up_w)
    sdt = T(L0_ffn_sh_down_w)
    rtw = T(L0_ffn_gate_weight)
    ebv = np.ascontiguousarray(np.broadcast_to(
        np.asarray(L0_ffn_e_score_correction_bias, np.float32).reshape(1, NE),
        (128, NE)))
    idn = np.eye(128, dtype=np.float32)

    nc = build_nc()
    in_maps = [
        _prep_core_inputs(c, x, wq, wk, wv, wo, gwt, uwt, dwt, sgt, sut, sdt,
                          rtw, ebv, idn)
        for c in range(8)
    ]
    res = run_bass_kernel_spmd(nc, in_maps, core_ids=list(range(8)))

    out = np.empty((B_SZ, SEQ, D), dtype=np.float32)
    for c in range(8):
        b, half = c >> 1, c & 1
        arr = np.asarray(res.results[c]["out"])
        rows = arr.reshape(D, 2048).T
        for j in range(NSLOT):
            g = slot_global(j, half)
            out[b, 128 * g:128 * (g + 1), :] = rows[128 * j:128 * (j + 1)]
    return out


# version 02
# speedup vs baseline: 1.2773x; 1.2773x over previous
import math
import contextlib
from collections import defaultdict

import numpy as np
import ml_dtypes

import concourse.bass as bass
import concourse.mybir as mybir
from concourse.bass_utils import run_bass_kernel_spmd

F32 = mybir.dt.float32
BF16 = mybir.dt.bfloat16
I32 = mybir.dt.int32
AF = mybir.ActivationFunctionType
ALU = mybir.AluOpType
AX = mybir.AxisListType

B_SZ = 4
SEQ = 4096
D = 2048
DC = 16
H = 16
DH = 128
NE = 16
FM = 1024
FSH = 2048
TILE = 1024
NSLOT = 16
KSCALE = 1.0 / math.sqrt(DH)
NEG = -1.0e30

NQROT = 4
STREAM_OF = {'T': 'T', 'V': 'V', 'S': 'S', 'P': 'P', 'Y': 'Y'}
STEP = {'T': 1, 'V': 1, 'S': 1, 'P': 1, 'Y': 1}
for _i in range(NQROT):
    STREAM_OF[f'QP{_i}'] = 'P'
    STREAM_OF[f'QY{_i}'] = 'Y'
    STEP[f'QP{_i}'] = 16
    STEP[f'QY{_i}'] = 16
STREAMS = ['T', 'V', 'S', 'P', 'Y']
SEMKEYS = list(STEP)


def slopes():
    return [0.5 ** ((h + 1) / 2.0) for h in range(H)]


def slot_global(j, half):
    cls = j // 2
    m = j % 2
    return 4 * cls + (3 * m if half == 0 else 1 + m)


class Bld:

    def __init__(self, nc, es):
        self.nc = nc
        self.sems = {k: es.enter_context(nc.semaphore("sem" + k)) for k in SEMKEYS}
        self.prog = {s: [] for s in STREAMS}
        self.cnt = defaultdict(int)
        self.know = {s: defaultdict(int) for s in STREAMS}
        self.snap = {k: [] for k in SEMKEYS}
        self.lastw = {}
        self.readers = defaultdict(dict)
        self.nwaits = 0
        self.qrr = {'P': 0, 'Y': 0}

    def _deps_for(self, reads, writes):
        deps = {}
        def add(k, c):
            if deps.get(k, 0) < c:
                deps[k] = c
        for r in reads:
            lw = self.lastw.get(r)
            if lw:
                add(*lw)
        for w in writes:
            lw = self.lastw.get(w)
            if lw:
                add(*lw)
            for k, c in self.readers.get(w, {}).items():
                add(k, c)
        return deps

    def emit(self, E, fname, *args, reads=(), writes=(), extra_deps=(), **kwargs):
        stream = STREAM_OF[E]
        deps = self._deps_for(reads, writes)
        for k, c in extra_deps:
            if deps.get(k, 0) < c:
                deps[k] = c
        waits = []
        kn = self.know[stream]
        for F, c in sorted(deps.items()):
            if kn.get(F, 0) >= c:
                continue
            sn = self.snap[F][c // STEP[F] - 1]
            for kk, vv in sn.items():
                if kn.get(kk, 0) < vv:
                    kn[kk] = vv
            if kn.get(F, 0) < c:
                kn[F] = c
            waits.append((F, c))
            self.nwaits += 1
        self.cnt[E] += STEP[E]
        c = self.cnt[E]
        sn = dict(kn)
        sn[E] = c
        self.snap[E].append(sn)
        if stream == 'T' == E:
            kn['T'] = c
        self.prog[stream].append((fname, args, kwargs, waits, E))
        for w in writes:
            self.lastw[w] = (E, c)
            self.readers[w] = {}
        for r in reads:
            self.readers[r][E] = c
        return c

    def barrier(self):
        for s in STREAMS:
            kn = self.know[s]
            waits = []
            for F in SEMKEYS:
                c = self.cnt[F]
                if c and kn.get(F, 0) < c:
                    sn = self.snap[F][c // STEP[F] - 1]
                    for kk, vv in sn.items():
                        if kn.get(kk, 0) < vv:
                            kn[kk] = vv
                    kn[F] = c
                    waits.append((F, c))
            if waits:
                self.prog[s].append((None, (), {}, waits, None))

    def dma(self, q, out, in_, reads=(), writes=(), **kw):
        idx = self.qrr[q]
        self.qrr[q] = (idx + 1) % NQROT
        E = f"Q{q}{idx}"
        ed = [(E, self.cnt[E])] if self.cnt[E] else []
        return self.emit(E, 'dma_start', reads=reads, writes=writes,
                         extra_deps=ed, out=out, in_=in_, **kw)

    def replay(self, block):
        holders = {}

        def run(stream):
            def body(eng):
                holders[stream] = eng
                for fname, args, kwargs, waits, E in self.prog[stream]:
                    for F, c in waits:
                        eng.wait_ge(self.sems[F], c)
                    if fname is None:
                        continue
                    inst = getattr(eng, fname)(*args, **kwargs)
                    inst.then_inc(self.sems[E], STEP[E])
            return body

        block.tensor(run('T'))
        block.vector(run('V'))
        block.scalar(run('S'))
        block.gpsimd(run('P'))
        block.sync(run('Y'))



def emit_attention(B, sb, sa, pa, pt, tau, heads, xnt, xntn, oall, oalln,
                   kscr, vscr, wq_t, masks, idt):
    q2 = [sb(f"q2{tau}{i}", [128, TILE], BF16, sa) for i in range(2)]
    kh = [sb(f"kh{tau}{i}", [128, SEQ], BF16, sa) for i in range(2)]
    vh = [sb(f"vh{tau}{i}", [128, 32, 128], BF16, sa) for i in range(2)]
    bsb = sb(f"bsb{tau}", [128, SEQ], F32, sa)
    scs = [sb(f"scs{tau}{i}", [128, SEQ], F32, sa) for i in range(2)]
    prb2 = [sb(f"prbb{tau}{i}", [128, SEQ], BF16, sa) for i in range(2)]
    ptb = sb(f"ptbuf{tau}", [128, 32, 256], BF16, sa)
    sums2 = [sb(f"sums{tau}{i}", [128, 8], F32, sa) for i in range(2)]
    asta = [sb(f"asta{tau}{i}", [128, 4], F32, sa) for i in range(2)]
    wqb = [sb(f"wq{tau}{i}", [128, DC, 128], BF16, sa) for i in range(2)]
    mskt = sb(f"mskt{tau}", [128, 8, 512], BF16, sa)
    B.dma('P', mskt[:],
          masks[tau * 8:(tau + 1) * 8].rearrange("s p k -> p s k"),
          writes=['mskt'])
    kwt = 512 * (4 * tau + 4)
    slope = slopes()

    def pre_dma(h):
        par = h % 2
        B.dma('P', wqb[par][:],
              wq_t[:, h * 128:(h + 1) * 128].rearrange("(c p) o -> p c o", p=128),
              writes=[f"wq{par}"])
        B.dma('Y', kh[par][:, 0:kwt], kscr[h, :, 0:kwt],
              reads=['kscr'], writes=[f"kh{par}"])
        B.dma('Y', vh[par][:, 0:kwt // 128, :],
              vscr[0:kwt, h * 128:(h + 1) * 128]
              .rearrange("(a p) e -> p a e", p=128),
              reads=['vscr'], writes=[f"vh{par}"])

    def pre_compute(h):
        par = h % 2
        for dc in range(DC):
            for qc in range(2):
                B.emit('T', 'matmul', pa[2 + qc][:], wqb[par][:, dc, :],
                       xnt[:, dc, qc * 512:(qc + 1) * 512],
                       start=(dc == 0), stop=(dc == DC - 1),
                       reads=[f"wq{par}", xntn], writes=[f"pa{2 + qc}"])
        for qc in range(2):
            B.emit('S', 'copy', q2[par][:, qc * 512:(qc + 1) * 512],
                   pa[2 + qc][:], reads=[f"pa{2 + qc}"], writes=[f"q2{par}"])
        B.emit('P', 'iota', bsb[:, 0:kwt].bitcast(I32), [[1, kwt]],
               channel_multiplier=0, writes=['bsb'])
        B.emit('V', 'tensor_single_scalar', bsb[:, 0:kwt],
               bsb[:, 0:kwt].bitcast(I32), slope[h], ALU.mult,
               reads=['bsb'], writes=['bsb'])

    def cls_of(j):
        return 4 * tau + j // 2 + 1

    def stageA(h, j):
        par, sp = h % 2, j % 2
        kw = 512 * cls_of(j)
        for ch in range(cls_of(j)):
            bank = pa[ch % 2]
            bname = f"pa{ch % 2}"
            B.emit('T', 'matmul', bank[:], q2[par][:, j * 128:(j + 1) * 128],
                   kh[par][:, ch * 512:(ch + 1) * 512], start=True, stop=True,
                   reads=[f"q2{par}", f"kh{par}"], writes=[bname])
            B.emit('V', 'tensor_tensor', scs[sp][:, ch * 512:(ch + 1) * 512],
                   bank[:], bsb[:, ch * 512:(ch + 1) * 512], ALU.add,
                   reads=[bname, 'bsb'], writes=[f"scs{sp}"])
        B.emit('V', 'tensor_tensor', scs[sp][:, kw - 512:kw],
               scs[sp][:, kw - 512:kw], mskt[:, j, :], ALU.add,
               reads=[f"scs{sp}", 'mskt'], writes=[f"scs{sp}"])
        B.emit('V', 'reduce_max', asta[sp][:, 0:1], scs[sp][:, 0:kw], AX.X,
               negate=True, reads=[f"scs{sp}"], writes=[f"asta{sp}"])

    def stageB(h, j):
        sp = j % 2
        cls = cls_of(j)
        kw = 512 * cls
        for ch in range(cls):
            B.emit('S', 'activation', prb2[sp][:, ch * 512:(ch + 1) * 512],
                   scs[sp][:, ch * 512:(ch + 1) * 512], AF.Exp,
                   bias=asta[sp][:, 0:1], accum_out=sums2[sp][:, ch:ch + 1],
                   reads=[f"scs{sp}", f"asta{sp}"],
                   writes=[f"prbb{sp}", f"sums{sp}"])
        B.emit('V', 'reduce_sum', asta[sp][:, 1:2], sums2[sp][:, 0:cls], AX.X,
               reads=[f"sums{sp}"], writes=[f"asta{sp}"])
        B.emit('V', 'reciprocal', asta[sp][:, 2:3], asta[sp][:, 1:2],
               reads=[f"asta{sp}"], writes=[f"asta{sp}"])
        B.emit('V', 'tensor_scalar', prb2[sp][:, 0:kw], prb2[sp][:, 0:kw],
               asta[sp][:, 2:3], None, ALU.mult,
               reads=[f"prbb{sp}", f"asta{sp}"], writes=[f"prbb{sp}"])

    def stageC(h, j):
        par, sp = h % 2, j % 2
        m, pr = j % 2, j // 2
        nkt = 512 * cls_of(j) // 128
        for g8 in range((nkt + 7) // 8):
            gn = min(8, nkt - g8 * 8)
            bank = pt[g8 % 2]
            bname = f"pt{g8 % 2}"
            for k in range(gn):
                kt = g8 * 8 + k
                B.emit('T', 'transpose', bank[:, k * 128:(k + 1) * 128],
                       prb2[sp][:, kt * 128:(kt + 1) * 128], idt[:],
                       reads=[f"prbb{sp}", 'idt'], writes=[bname])
            B.emit('S', 'copy',
                   ptb[:, g8 * 8:g8 * 8 + gn, m * 128:(m + 1) * 128],
                   bank[:, 0:gn * 128].rearrange("p (a b) -> p a b", b=128),
                   reads=[bname], writes=['ptbuf'])
        if m == 1:
            for kt in range(nkt):
                B.emit('T', 'matmul', pa[4][:, 0:256], vh[par][:, kt, :],
                       ptb[:, kt, :], start=(kt == 0), stop=(kt == nkt - 1),
                       reads=[f"vh{par}", 'ptbuf'], writes=['pa4'])
            B.emit('S', 'copy', oall[:, h, pr * 256:(pr + 1) * 256],
                   pa[4][:, 0:256], reads=['pa4'], writes=[oalln])

    pre_dma(0)
    pre_compute(0)
    total = heads * 8
    for t_ in range(total + 2):
        if t_ < total:
            stageA(*divmod(t_, 8))
        if 1 <= t_ <= total:
            stageB(*divmod(t_ - 1, 8))
        if 2 <= t_ <= total + 1:
            stageC(*divmod(t_ - 2, 8))
        if t_ < total:
            hA, jA = divmod(t_, 8)
            if jA == 5 and hA + 1 < heads:
                pre_dma(hA + 1)
            if jA == 7 and hA + 1 < heads:
                pre_compute(hA + 1)
    B.barrier()


def build_nc(cfg=None):
    cfg = dict(cfg or {})
    p1_cols = cfg.get('p1_cols', 8)
    tiles = cfg.get('tiles', [0, 1])
    heads = cfg.get('heads', H)
    experts = cfg.get('experts', NE)
    do_attn = cfg.get('do_attn', True)
    do_moe = cfg.get('do_moe', True)
    do_shared = cfg.get('do_shared', True)
    do_p1 = cfg.get('do_p1', True)

    nc = bass.Bass("TRN2", target_bir_lowering=False)

    dr = {}
    def din(name, shape, dt=F32):
        dr[name] = nc.dram_tensor(name, shape, dt, kind="ExternalInput").ap()
        return dr[name]

    xseq = din("xseq", [SEQ, D])
    xown = din("xown", [2048, D])
    wq_t = din("wq_t", [D, D])
    wk_t = din("wk_t", [D, D])
    wv_t = din("wv_t", [D, D])
    wo_t = din("wo_t", [D, D])
    gw_t = din("gw_t", [NE, D, FM])
    uw_t = din("uw_t", [NE, D, FM])
    dw_t = din("dw_t", [NE, FM, D])
    sg_t = din("sg_t", [D, FSH])
    su_t = din("su_t", [D, FSH])
    sd_t = din("sd_t", [FSH, D])
    rt = din("rt", [D, NE])
    ebi = din("eb", [128, NE])
    ident_in = din("ident", [128, 128])
    masks = din("masks", [NSLOT, 128, 512])

    out_d = nc.dram_tensor("out", [DC, 128, 2048], F32, kind="ExternalOutput").ap()
    kscr = nc.dram_tensor("kscr", [H, DH, SEQ], BF16, kind="Internal").ap()
    vscr = nc.dram_tensor("vscr", [SEQ, D], BF16, kind="Internal").ap()

    with contextlib.ExitStack() as top:
        B = Bld(nc, top)
        sb = lambda n, shp, dt, st=top, side=None: st.enter_context(nc.sbuf_tensor(n, shp, dt, side=side))
        pa = [top.enter_context(nc.psum_tensor(f"pa{i}", [128, 512], F32))
              for i in range(6)]
        pt = [top.enter_context(nc.psum_tensor(f"pt{i}", [128, 1024], BF16))
              for i in range(2)]

        idt = sb("idt", [128, 128], BF16)
        idtf = sb("idtf", [128, 128], F32)
        ebbc = sb("ebbc", [128, NE], F32)
        B.dma('P', idt[:], ident_in, writes=['idt'])
        B.dma('Y', idtf[:], ident_in, writes=['idtf'])
        B.dma('Y', ebbc[:], ebi, writes=['ebbc'])

        def emit_ln(stx, par, x_rows_ap, dst, dst_name, dst_toff, bufs):
            xt = bufs['xt'][par]; xtn = f"xt{par}"
            sq = bufs['sq'][par]; sqn = f"sq{par}"
            xn = bufs['xn'][par]; xnn = f"xn{par}"
            stt_ = bufs['st'][par]; stn = f"lst{par}"
            B.dma('Y', xt[:], x_rows_ap, writes=[xtn])
            B.emit('V', 'reduce_sum', stt_[:, 0:1], xt[:], AX.X,
                   reads=[xtn], writes=[stn])
            B.emit('S', 'activation', sq[:], xt[:], AF.Square,
                   accum_out=stt_[:, 2:3], reads=[xtn], writes=[sqn, stn])
            B.emit('V', 'tensor_scalar', stt_[:, 1:2], stt_[:, 0:1],
                   -1.0 / D, None, ALU.mult, reads=[stn], writes=[stn])
            B.emit('V', 'tensor_scalar', stt_[:, 3:4], stt_[:, 2:3],
                   1.0 / D, None, ALU.mult, reads=[stn], writes=[stn])
            B.emit('V', 'tensor_tensor', stt_[:, 4:5], stt_[:, 1:2], stt_[:, 1:2],
                   ALU.mult, reads=[stn], writes=[stn])
            B.emit('V', 'tensor_tensor', stt_[:, 5:6], stt_[:, 3:4], stt_[:, 4:5],
                   ALU.subtract, reads=[stn], writes=[stn])
            B.emit('V', 'tensor_scalar', stt_[:, 5:6], stt_[:, 5:6], 1e-5, None,
                   ALU.add, reads=[stn], writes=[stn])
            B.emit('S', 'activation', stt_[:, 6:7], stt_[:, 5:6], AF.Sqrt,
                   reads=[stn], writes=[stn])
            B.emit('V', 'reciprocal', stt_[:, 7:8], stt_[:, 6:7],
                   reads=[stn], writes=[stn])
            B.emit('V', 'tensor_scalar', xn[:], xt[:], stt_[:, 1:2], stt_[:, 7:8],
                   ALU.add, ALU.mult, reads=[xtn, stn], writes=[xnn])
            for g8 in range(2):
                ptb = pt[g8]
                ptn = f"pt{g8}"
                for k in range(8):
                    dc = g8 * 8 + k
                    B.emit('T', 'transpose', ptb[:, k * 128:(k + 1) * 128],
                           xn[:, dc * 128:(dc + 1) * 128], idt[:],
                           reads=[xnn, 'idt'], writes=[ptn])
                B.emit('V', 'tensor_copy',
                       dst[:, g8 * 8:(g8 + 1) * 8, dst_toff:dst_toff + 128],
                       ptb[:].rearrange("p (a b) -> p a b", a=8),
                       reads=[ptn], writes=[dst_name])

        if do_p1:
            with contextlib.ExitStack() as s1:
                wk = sb("wk", [128, DC, D], BF16, s1)
                wv = sb("wv", [128, DC, D], BF16, s1)
                xnf = sb("xnf", [128, DC, 512], BF16, s1)
                kcol = sb("kcol", [128, H, 512], BF16, s1)
                bufs = {
                    'xt': [sb(f"p1xt{i}", [128, D], F32, s1) for i in range(2)],
                    'sq': [sb(f"p1sq{i}", [128, D], BF16, s1) for i in range(2)],
                    'xn': [sb(f"p1xn{i}", [128, D], BF16, s1) for i in range(2)],
                    'st': [sb(f"p1st{i}", [128, 8], F32, s1) for i in range(2)],
                }
                vst = [sb(f"vst{i}", [128, D], BF16, s1) for i in range(2)]
                B.dma('P', wk[:], wk_t.rearrange("(c p) o -> p c o", p=128),
                      writes=['wk'])
                B.dma('P', wv[:], wv_t.rearrange("(c p) o -> p c o", p=128),
                      writes=['wv'])
                for col in range(p1_cols):
                    for tt in range(4):
                        r0 = col * 512 + tt * 128
                        emit_ln(s1, tt % 2, xseq[r0:r0 + 128, :],
                                xnf, 'xnf', tt * 128, bufs)
                    for h in range(H):
                        bank = pa[h % 2]
                        bname = f"pa{h % 2}"
                        for dc in range(DC):
                            B.emit('T', 'matmul', bank[:],
                                   wk[:, dc, h * 128:(h + 1) * 128],
                                   xnf[:, dc, :],
                                   start=(dc == 0), stop=(dc == DC - 1),
                                   reads=['wk', 'xnf'], writes=[bname])
                        B.emit('S', 'mul', kcol[:, h, :], bank[:], KSCALE,
                               reads=[bname], writes=['kcol'])
                    B.dma('Y',
                          kscr[:, :, col * 512:(col + 1) * 512]
                          .rearrange("h p k -> p h k"),
                          kcol[:], reads=['kcol'], writes=['kscr'])
                    for tt in range(4):
                        vpar = tt % 2
                        vb = vst[vpar]
                        vbn = f"vst{vpar}"
                        for ec in range(4):
                            bank = pa[2 + ec % 2]
                            bname = f"pa{2 + ec % 2}"
                            for dc in range(DC):
                                B.emit('T', 'matmul', bank[:],
                                       xnf[:, dc, tt * 128:(tt + 1) * 128],
                                       wv[:, dc, ec * 512:(ec + 1) * 512],
                                       start=(dc == 0), stop=(dc == DC - 1),
                                       reads=['xnf', 'wv'], writes=[bname])
                            B.emit('V', 'tensor_copy',
                                   vb[:, ec * 512:(ec + 1) * 512], bank[:],
                                   reads=[bname], writes=[vbn])
                        r0 = col * 512 + tt * 128
                        B.dma('Y', vscr[r0:r0 + 128, :], vb[:],
                              reads=[vbn], writes=['vscr'])
                B.barrier()

        acc_live = False
        for tau in tiles:
            with contextlib.ExitStack() as s2:
                xnt = sb(f"xnt{tau}", [128, DC, TILE], BF16, s2)
                xntn = f"xnt{tau}"
                wall = sb(f"wall{tau}", [128, 8, NE], F32, s2)
                walln = f"wall{tau}"
                with contextlib.ExitStack() as sl:
                    bufs = {
                        'xt': [sb(f"t{tau}xt{i}", [128, D], F32, sl) for i in range(2)],
                        'sq': [sb(f"t{tau}sq{i}", [128, D], BF16, sl) for i in range(2)],
                        'xn': [sb(f"t{tau}xn{i}", [128, D], BF16, sl) for i in range(2)],
                        'st': [sb(f"t{tau}st{i}", [128, 8], F32, sl) for i in range(2)],
                    }
                    for ts in range(8):
                        r0 = tau * 1024 + ts * 128
                        emit_ln(sl, ts % 2, xown[r0:r0 + 128, :],
                                xnt, xntn, ts * 128, bufs)
                    B.barrier()

                with contextlib.ExitStack() as sr:
                    rtb = sb(f"rtb{tau}", [128, DC, NE], BF16, sr)
                    lg = sb(f"lg{tau}", [128, NE], F32, sr)
                    sc = sb(f"sc{tau}", [128, NE], F32, sr)
                    sfc = sb(f"sfc{tau}", [128, NE], F32, sr)
                    grp6 = sb(f"grp6{tau}", [128, 4, 6], F32, sr)
                    gs = sb(f"gs{tau}", [128, 4], F32, sr)
                    gsc = sb(f"gsc{tau}", [128, 12], F32, sr)
                    is1 = sb(f"is1{tau}", [128, 4], F32, sr)
                    is2 = sb(f"is2{tau}", [128, 4], F32, sr)
                    gm = sb(f"gm{tau}", [128, 4], F32, sr)
                    mk = sb(f"mk{tau}", [128, NE], F32, sr)
                    selm = sb(f"selm{tau}", [128, NE], F32, sr)
                    isr = sb(f"isr{tau}", [128, NE], F32, sr)
                    wraw = sb(f"wraw{tau}", [128, NE], F32, sr)
                    B.dma('P', rtb[:], rt.rearrange("(c p) n -> p c n", p=128),
                          writes=['rtb'])
                    for ts in range(8):
                        bank = pa[0]
                        for dc in range(DC):
                            B.emit('T', 'matmul', bank[:, 0:NE],
                                   xnt[:, dc, ts * 128:(ts + 1) * 128],
                                   rtb[:, dc, :],
                                   start=(dc == 0), stop=(dc == DC - 1),
                                   reads=[xntn, 'rtb'], writes=['pa0'])
                        B.emit('V', 'tensor_copy', lg[:], bank[:, 0:NE],
                               reads=['pa0'], writes=['lg'])
                        B.emit('S', 'activation', sc[:], lg[:], AF.Sigmoid,
                               reads=['lg'], writes=['sc'])
                        B.emit('V', 'tensor_tensor', sfc[:], sc[:], ebbc[:],
                               ALU.add, reads=['sc', 'ebbc'], writes=['sfc'])
                        pairs = [(0, 1), (0, 2), (0, 3), (1, 2), (1, 3), (2, 3)]
                        for pi, (a, b) in enumerate(pairs):
                            B.emit('V', 'tensor_tensor', grp6[:, :, pi],
                                   sfc[:].rearrange("p (g e) -> p g e", e=4)[:, :, a],
                                   sfc[:].rearrange("p (g e) -> p g e", e=4)[:, :, b],
                                   ALU.add, reads=['sfc'], writes=['grp6'])
                        B.emit('V', 'reduce_max', gs[:].rearrange("p (g o) -> p g o", o=1),
                               grp6[:], AX.X, reads=['grp6'], writes=['gs'])
                        B.emit('V', 'reduce_max', gsc[:, 0:1], gs[:], AX.X,
                               reads=['gs'], writes=['gsc'])
                        B.emit('V', 'tensor_scalar', is1[:], gs[:], gsc[:, 0:1],
                               None, ALU.is_ge, reads=['gs', 'gsc'], writes=['is1'])
                        B.emit('V', 'scalar_tensor_tensor', gm[:], is1[:], NEG,
                               gs[:], ALU.mult, ALU.add,
                               reads=['is1', 'gs'], writes=['gm'])
                        B.emit('V', 'reduce_max', gsc[:, 1:2], gm[:], AX.X,
                               reads=['gm'], writes=['gsc'])
                        B.emit('V', 'tensor_scalar', is2[:], gm[:], gsc[:, 1:2],
                               None, ALU.is_ge, reads=['gm', 'gsc'], writes=['is2'])
                        B.emit('V', 'tensor_tensor', gm[:], is1[:], is2[:],
                               ALU.add, reads=['is1', 'is2'], writes=['gm'])
                        B.emit('V', 'tensor_scalar', gm[:], gm[:], -1.0, -NEG,
                               ALU.add, ALU.mult, reads=['gm'], writes=['gm'])
                        for g in range(4):
                            B.emit('V', 'tensor_scalar', mk[:, 4 * g:4 * g + 4],
                                   sfc[:, 4 * g:4 * g + 4], gm[:, g:g + 1], None,
                                   ALU.add, reads=['sfc', 'gm'], writes=['mk'])
                        B.emit('V', 'memset', selm[:], 0.0, writes=['selm'])
                        for r in range(4):
                            B.emit('V', 'reduce_max', gsc[:, 2:3], mk[:], AX.X,
                                   reads=['mk'], writes=['gsc'])
                            B.emit('V', 'tensor_scalar', isr[:], mk[:],
                                   gsc[:, 2:3], None, ALU.is_ge,
                                   reads=['mk', 'gsc'], writes=['isr'])
                            B.emit('V', 'tensor_tensor', selm[:], selm[:], isr[:],
                                   ALU.max, reads=['selm', 'isr'], writes=['selm'])
                            if r < 3:
                                B.emit('V', 'scalar_tensor_tensor', mk[:], isr[:],
                                       NEG, mk[:], ALU.mult, ALU.add,
                                       reads=['isr', 'mk'], writes=['mk'])
                        B.emit('V', 'tensor_tensor', wraw[:], sc[:], selm[:],
                               ALU.mult, reads=['sc', 'selm'], writes=['wraw'])
                        B.emit('V', 'reduce_sum', gsc[:, 3:4], wraw[:], AX.X,
                               reads=['wraw'], writes=['gsc'])
                        B.emit('V', 'tensor_scalar', gsc[:, 4:5], gsc[:, 3:4],
                               1e-20, None, ALU.add, reads=['gsc'], writes=['gsc'])
                        B.emit('V', 'reciprocal', gsc[:, 5:6], gsc[:, 4:5],
                               reads=['gsc'], writes=['gsc'])
                        B.emit('V', 'tensor_scalar', wall[:, ts, :], wraw[:],
                               gsc[:, 5:6], 2.5, ALU.mult, ALU.mult,
                               reads=['wraw', 'gsc'], writes=[walln])
                    B.barrier()

                s2b = s2.enter_context(contextlib.ExitStack())
                oall = sb(f"oall{tau}", [128, H, TILE], BF16, s2b)
                oalln = f"oall{tau}"
                if do_attn:
                    B.emit('V', 'memset', oall[:], 0.0, writes=[oalln])
                    with contextlib.ExitStack() as sa:
                        emit_attention(B, sb, sa, pa, pt, tau, heads, xnt,
                                       xntn, oall, oalln, kscr, vscr, wq_t,
                                       masks, idt)

                acc = sb(f"acc{tau}", [128, DC, TILE], F32, s2, side="right")
                accn = f"acc{tau}"
                with contextlib.ExitStack() as sw:
                    xtw = [sb(f"w{tau}xt{i}", [128, D], F32, sw) for i in range(2)]
                    wob = [sb(f"wo{tau}{i}", [128, DC, 128], BF16, sw)
                           for i in range(2)]
                    for ts in range(8):
                        par = ts % 2
                        r0 = tau * 1024 + ts * 128
                        B.dma('Y', xtw[par][:], xown[r0:r0 + 128, :],
                              writes=[f"wxt{par}"])
                        for g4 in range(4):
                            bank = pa[3 + g4 % 3]
                            bname = f"pa{3 + g4 % 3}"
                            for k in range(4):
                                dc = g4 * 4 + k
                                B.emit('T', 'transpose',
                                       bank[:, k * 128:(k + 1) * 128],
                                       xtw[par][:, dc * 128:(dc + 1) * 128],
                                       idtf[:],
                                       reads=[f"wxt{par}", 'idtf'], writes=[bname])
                            B.emit('V', 'tensor_copy',
                                   acc[:, g4 * 4:(g4 + 1) * 4, ts * 128:(ts + 1) * 128],
                                   bank[:].rearrange("p (a b) -> p a b", a=4),
                                   reads=[bname], writes=[accn])
                    if do_attn:
                        for oc in range(DC):
                            wpar = oc % 2
                            if oc == 0:
                                B.dma('P', wob[0][:],
                                      wo_t[:, 0:128].rearrange("(c p) o -> p c o", p=128),
                                      writes=['wo0'])
                            if oc + 1 < DC:
                                B.dma('P', wob[(oc + 1) % 2][:],
                                      wo_t[:, (oc + 1) * 128:(oc + 2) * 128]
                                      .rearrange("(c p) o -> p c o", p=128),
                                      writes=[f"wo{(oc + 1) % 2}"])
                            for qc in range(2):
                                bank = pa[qc]
                                bname = f"pa{qc}"
                                for ec in range(H):
                                    B.emit('T', 'matmul', bank[:],
                                           wob[wpar][:, ec, :],
                                           oall[:, ec, qc * 512:(qc + 1) * 512],
                                           start=(ec == 0), stop=(ec == H - 1),
                                           reads=[f"wo{wpar}", oalln],
                                           writes=[bname])
                                B.emit('V', 'tensor_tensor',
                                       acc[:, oc, qc * 512:(qc + 1) * 512],
                                       bank[:],
                                       acc[:, oc, qc * 512:(qc + 1) * 512],
                                       ALU.add,
                                       reads=[bname, accn], writes=[accn])
                    B.barrier()

                s2b.close()

                if do_moe:
                    with contextlib.ExitStack() as sm:
                        wgb = [sb(f"wg{tau}{i}", [128, DC, 512], BF16, sm)
                               for i in range(2)]
                        wub = [sb(f"wu{tau}{i}", [128, DC, 512], BF16, sm)
                               for i in range(2)]
                        dwb = [sb(f"dw{tau}{i}", [128, 8, 128], BF16, sm)
                               for i in range(2)]
                        act = sb(f"act{tau}", [128, 8, TILE], BF16, sm)
                        sg_ = [sb(f"sg_{tau}{i}", [128, 512], BF16, sm)
                               for i in range(2)]
                        ast = [sb(f"ast{tau}{i}", [128, 512], BF16, sm)
                               for i in range(2)]

                        def gu_dma(n, fc, par):
                            B.dma('P', wgb[par][:],
                                  gw_t[n, :, fc * 512:(fc + 1) * 512]
                                  .rearrange("(c p) f -> p c f", p=128),
                                  writes=[f"wg{par}"])
                            B.dma('P', wub[par][:],
                                  uw_t[n, :, fc * 512:(fc + 1) * 512]
                                  .rearrange("(c p) f -> p c f", p=128),
                                  writes=[f"wu{par}"])

                        def dw_dma(n, hc, par):
                            B.dma('P', dwb[par][:],
                                  dw_t[n, :, hc * 128:(hc + 1) * 128]
                                  .rearrange("(c p) h2 -> p c h2", p=128),
                                  writes=[f"dw{par}"])

                        gu_dma(0, 0, 0)
                        for ni in range(experts):
                            n = ni
                            def moe_transp(fc, ts):
                                spar = ts % 2
                                bank = pt[ts % 2]
                                bname = f"pt{ts % 2}"
                                for ft in range(4):
                                    B.emit('T', 'transpose',
                                           bank[:, ft * 128:(ft + 1) * 128],
                                           ast[spar][:, ft * 128:(ft + 1) * 128],
                                           idt[:],
                                           reads=[f"ast{spar}", 'idt'],
                                           writes=[bname])
                                B.emit('V', 'tensor_copy',
                                       act[:, fc * 4:fc * 4 + 4,
                                           ts * 128:(ts + 1) * 128],
                                       bank[:, 0:512].rearrange(
                                           "p (a b) -> p a b", a=4),
                                       reads=[bname], writes=['act'])

                            for fc in range(2):
                                gpar = fc % 2
                                if fc + 1 < 2:
                                    gu_dma(n, fc + 1, (fc + 1) % 2)
                                elif ni + 1 < experts:
                                    gu_dma(ni + 1, 0, 0)
                                for ts in range(9):
                                    if ts < 8:
                                        pg, pu = pa[ts % 2], pa[2 + ts % 2]
                                        pgn = f"pa{ts % 2}"
                                        pun = f"pa{2 + ts % 2}"
                                        for dc in range(DC):
                                            B.emit('T', 'matmul', pg[:],
                                                   xnt[:, dc, ts * 128:(ts + 1) * 128],
                                                   wgb[gpar][:, dc, :],
                                                   start=(dc == 0), stop=(dc == DC - 1),
                                                   reads=[xntn, f"wg{gpar}"],
                                                   writes=[pgn])
                                            B.emit('T', 'matmul', pu[:],
                                                   xnt[:, dc, ts * 128:(ts + 1) * 128],
                                                   wub[gpar][:, dc, :],
                                                   start=(dc == 0), stop=(dc == DC - 1),
                                                   reads=[xntn, f"wu{gpar}"],
                                                   writes=[pun])
                                        spar = ts % 2
                                        B.emit('S', 'activation', sg_[spar][:], pg[:],
                                               AF.Silu, reads=[pgn],
                                               writes=[f"sg_{spar}"])
                                        B.emit('V', 'scalar_tensor_tensor',
                                               ast[spar][:], pu[:],
                                               wall[:, ts, n:n + 1], sg_[spar][:],
                                               ALU.mult, ALU.mult,
                                               reads=[pun, walln, f"sg_{spar}"],
                                               writes=[f"ast{spar}"])
                                    if ts >= 1:
                                        moe_transp(fc, ts - 1)
                            dw_dma(n, 0, 0)
                            for hc in range(DC):
                                dpar = hc % 2
                                if hc + 1 < DC:
                                    dw_dma(n, hc + 1, (hc + 1) % 2)
                                elif ni + 1 < experts:
                                    pass
                                for tc in range(2):
                                    bank = pa[4 + tc]
                                    bname = f"pa{4 + tc}"
                                    for fcc in range(8):
                                        B.emit('T', 'matmul', bank[:],
                                               dwb[dpar][:, fcc, :],
                                               act[:, fcc, tc * 512:(tc + 1) * 512],
                                               start=(fcc == 0), stop=(fcc == 7),
                                               reads=[f"dw{dpar}", 'act'],
                                               writes=[bname])
                                    B.emit('V', 'tensor_tensor',
                                           acc[:, hc, tc * 512:(tc + 1) * 512],
                                           bank[:],
                                           acc[:, hc, tc * 512:(tc + 1) * 512],
                                           ALU.add,
                                           reads=[bname, accn], writes=[accn])
                        B.barrier()

                if do_shared:
                    with contextlib.ExitStack() as ss:
                        acts = sb(f"acts{tau}", [128, 16, TILE], BF16, ss)
                        sgb = [sb(f"sgb{tau}{i}", [128, DC, 128], BF16, ss)
                               for i in range(2)]
                        sub = [sb(f"sub{tau}{i}", [128, DC, 128], BF16, ss)
                               for i in range(2)]
                        sdb = [sb(f"sdb{tau}{i}", [128, 16, 128], BF16, ss)
                               for i in range(2)]
                        sgs = [sb(f"sgs{tau}{i}", [128, 512], BF16, ss)
                               for i in range(2)]

                        def sh_dma(fc, par):
                            B.dma('P', sgb[par][:],
                                  sg_t[:, fc * 128:(fc + 1) * 128]
                                  .rearrange("(c p) f -> p c f", p=128),
                                  writes=[f"sgb{par}"])
                            B.dma('P', sub[par][:],
                                  su_t[:, fc * 128:(fc + 1) * 128]
                                  .rearrange("(c p) f -> p c f", p=128),
                                  writes=[f"sub{par}"])

                        def sd_dma(hc, par):
                            B.dma('P', sdb[par][:],
                                  sd_t[:, hc * 128:(hc + 1) * 128]
                                  .rearrange("(c p) h2 -> p c h2", p=128),
                                  writes=[f"sdb{par}"])

                        sh_dma(0, 0)
                        for fc in range(16):
                            par = fc % 2
                            if fc + 1 < 16:
                                sh_dma(fc + 1, (fc + 1) % 2)
                            for tc in range(2):
                                pg, pu = pa[tc], pa[2 + tc]
                                pgn, pun = f"pa{tc}", f"pa{2 + tc}"
                                for dc in range(DC):
                                    B.emit('T', 'matmul', pg[:],
                                           sgb[par][:, dc, :],
                                           xnt[:, dc, tc * 512:(tc + 1) * 512],
                                           start=(dc == 0), stop=(dc == DC - 1),
                                           reads=[f"sgb{par}", xntn], writes=[pgn])
                                    B.emit('T', 'matmul', pu[:],
                                           sub[par][:, dc, :],
                                           xnt[:, dc, tc * 512:(tc + 1) * 512],
                                           start=(dc == 0), stop=(dc == DC - 1),
                                           reads=[f"sub{par}", xntn], writes=[pun])
                                B.emit('S', 'activation', sgs[tc][:], pg[:],
                                       AF.Silu, reads=[pgn], writes=[f"sgs{tc}"])
                                B.emit('V', 'scalar_tensor_tensor',
                                       acts[:, fc, tc * 512:(tc + 1) * 512],
                                       pu[:], 1.0, sgs[tc][:],
                                       ALU.mult, ALU.mult,
                                       reads=[pun, f"sgs{tc}"], writes=['acts'])
                        sd_dma(0, 0)
                        for hc in range(DC):
                            par = hc % 2
                            if hc + 1 < DC:
                                sd_dma(hc + 1, (hc + 1) % 2)
                            for tc in range(2):
                                bank = pa[4 + tc]
                                bname = f"pa{4 + tc}"
                                for fcc in range(16):
                                    B.emit('T', 'matmul', bank[:],
                                           sdb[par][:, fcc, :],
                                           acts[:, fcc, tc * 512:(tc + 1) * 512],
                                           start=(fcc == 0), stop=(fcc == 15),
                                           reads=[f"sdb{par}", 'acts'],
                                           writes=[bname])
                                B.emit('V', 'tensor_tensor',
                                       acc[:, hc, tc * 512:(tc + 1) * 512],
                                       bank[:],
                                       acc[:, hc, tc * 512:(tc + 1) * 512],
                                       ALU.add,
                                       reads=[bname, accn], writes=[accn])
                        B.barrier()

                B.dma('Y',
                      out_d[:, :, tau * 1024:(tau + 1) * 1024]
                      .rearrange("c p t -> p c t"),
                      acc[:], reads=[accn], writes=['out'])
                B.barrier()

        fin = [(k, B.cnt[k]) for k in SEMKEYS
               if k.startswith('Q') and B.cnt[k]]
        B.prog['Y'].append((None, (), {}, fin, None))

        with nc.Block() as block:
            B.replay(block)

    return nc



def _prep_core_inputs(core, x, wq, wk, wv, wo, gwt, uwt, dwt, sgt, sut, sdt,
                      rtw, ebv, idn, maskcache={}):
    b = core >> 1
    half = core & 1
    xb = np.ascontiguousarray(x[b])
    rows = []
    for j in range(NSLOT):
        g = slot_global(j, half)
        rows.append(xb[128 * g:128 * (g + 1)])
    xo = np.ascontiguousarray(np.concatenate(rows, axis=0))
    if half not in maskcache:
        mk = np.zeros((NSLOT, 128, 512), dtype=np.float32)
        for j in range(NSLOT):
            g = slot_global(j, half)
            cls = j // 2 + 1
            q = 128 * g + np.arange(128)[:, None]
            kk = 512 * (cls - 1) + np.arange(512)[None, :]
            mk[j] = np.where(kk <= q, 0.0, NEG).astype(np.float32)
        maskcache[half] = mk
    return {
        "xseq": xb, "xown": xo,
        "wq_t": wq, "wk_t": wk, "wv_t": wv, "wo_t": wo,
        "gw_t": gwt, "uw_t": uwt, "dw_t": dwt,
        "sg_t": sgt, "su_t": sut, "sd_t": sdt,
        "rt": rtw, "eb": ebv, "ident": idn, "masks": maskcache[half],
    }


def kernel(L0_attn_x, L0_attn_Wq, L0_attn_Wk, L0_attn_Wv, L0_attn_Wo,
           L0_ffn_gate_weight, L0_ffn_e_score_correction_bias,
           L0_ffn_gate_w, L0_ffn_up_w, L0_ffn_down_w,
           L0_ffn_sh_gate_w, L0_ffn_sh_up_w, L0_ffn_sh_down_w):
    x = np.asarray(L0_attn_x, dtype=np.float32)
    T = lambda a: np.ascontiguousarray(np.asarray(a, np.float32).T)
    wq, wk = T(L0_attn_Wq), T(L0_attn_Wk)
    wv, wo = T(L0_attn_Wv), T(L0_attn_Wo)
    gwt = np.ascontiguousarray(np.asarray(L0_ffn_gate_w, np.float32).transpose(0, 2, 1))
    uwt = np.ascontiguousarray(np.asarray(L0_ffn_up_w, np.float32).transpose(0, 2, 1))
    dwt = np.ascontiguousarray(np.asarray(L0_ffn_down_w, np.float32).transpose(0, 2, 1))
    sgt, sut = T(L0_ffn_sh_gate_w), T(L0_ffn_sh_up_w)
    sdt = T(L0_ffn_sh_down_w)
    rtw = T(L0_ffn_gate_weight)
    ebv = np.ascontiguousarray(np.broadcast_to(
        np.asarray(L0_ffn_e_score_correction_bias, np.float32).reshape(1, NE),
        (128, NE)))
    idn = np.eye(128, dtype=np.float32)

    nc = build_nc()
    in_maps = [
        _prep_core_inputs(c, x, wq, wk, wv, wo, gwt, uwt, dwt, sgt, sut, sdt,
                          rtw, ebv, idn)
        for c in range(8)
    ]
    res = run_bass_kernel_spmd(nc, in_maps, core_ids=list(range(8)))

    out = np.empty((B_SZ, SEQ, D), dtype=np.float32)
    for c in range(8):
        b, half = c >> 1, c & 1
        arr = np.asarray(res.results[c]["out"])
        rows = arr.reshape(D, 2048).T
        for j in range(NSLOT):
            g = slot_global(j, half)
            out[b, 128 * g:128 * (g + 1), :] = rows[128 * j:128 * (j + 1)]
    return out


# version 03 (final)
# speedup vs baseline: 1.3238x; 1.0364x over previous
import math
import contextlib
from collections import defaultdict

import numpy as np
import ml_dtypes

import concourse.bass as bass
import concourse.mybir as mybir
from concourse.bass_utils import run_bass_kernel_spmd

F32 = mybir.dt.float32
BF16 = mybir.dt.bfloat16
I32 = mybir.dt.int32
AF = mybir.ActivationFunctionType
ALU = mybir.AluOpType
AX = mybir.AxisListType

B_SZ = 4
SEQ = 4096
D = 2048
DC = 16
H = 16
DH = 128
NE = 16
FM = 1024
FSH = 2048
TILE = 1024
NSLOT = 16
KSCALE = 1.0 / math.sqrt(DH)
NEG = -1.0e30

NQROT = 4
STREAM_OF = {'T': 'T', 'V': 'V', 'S': 'S', 'P': 'P', 'Y': 'Y'}
STEP = {'T': 1, 'V': 1, 'S': 1, 'P': 1, 'Y': 1}
for _i in range(NQROT):
    STREAM_OF[f'QP{_i}'] = 'P'
    STREAM_OF[f'QY{_i}'] = 'Y'
    STEP[f'QP{_i}'] = 16
    STEP[f'QY{_i}'] = 16
STREAMS = ['T', 'V', 'S', 'P', 'Y']
SEMKEYS = list(STEP)


def slopes():
    return [0.5 ** ((h + 1) / 2.0) for h in range(H)]


def slot_global(j, half):
    cls = j // 2
    m = j % 2
    return 4 * cls + (3 * m if half == 0 else 1 + m)


class Bld:

    def __init__(self, nc, es):
        self.nc = nc
        self.sems = {k: es.enter_context(nc.semaphore("sem" + k)) for k in SEMKEYS}
        self.prog = {s: [] for s in STREAMS}
        self.cnt = defaultdict(int)
        self.know = {s: defaultdict(int) for s in STREAMS}
        self.snap = {k: [] for k in SEMKEYS}
        self.lastw = {}
        self.readers = defaultdict(dict)
        self.nwaits = 0
        self.qrr = {'P': 0, 'Y': 0}
        self.pending = {}

    def _deps_for(self, reads, writes):
        deps = {}
        def add(k, c):
            if deps.get(k, 0) < c:
                deps[k] = c
        for r in reads:
            lw = self.lastw.get(r)
            if lw:
                add(*lw)
        for w in writes:
            lw = self.lastw.get(w)
            if lw:
                add(*lw)
            for k, c in self.readers.get(w, {}).items():
                add(k, c)
        return deps

    def emit(self, E, fname, *args, reads=(), writes=(), extra_deps=(),
             inc=True, **kwargs):
        stream = STREAM_OF[E]
        deps = self._deps_for(reads, writes)
        for k, c in extra_deps:
            if deps.get(k, 0) < c:
                deps[k] = c
        waits = []
        kn = self.know[stream]
        for F, c in sorted(deps.items()):
            if F == 'T' == stream:
                continue
            if kn.get(F, 0) >= c:
                continue
            assert c // STEP[F] - 1 < len(self.snap[F]), \
                f"dep on un-closed group of {F}"
            sn = self.snap[F][c // STEP[F] - 1]
            for kk, vv in sn.items():
                if kn.get(kk, 0) < vv:
                    kn[kk] = vv
            if kn.get(F, 0) < c:
                kn[F] = c
            waits.append((F, c))
            self.nwaits += 1
        if inc:
            self.cnt[E] += STEP[E]
            c = self.cnt[E]
            sn = dict(kn)
            sn[E] = c
            self.snap[E].append(sn)
            self.pending.pop(E, None)
        else:
            c = self.cnt[E] + STEP[E]
            self.pending[E] = c
        if stream == 'T' == E:
            kn['T'] = min(c, self.cnt[E])
        self.prog[stream].append((fname, args, kwargs, waits, E if inc else None))
        for w in writes:
            self.lastw[w] = (E, c)
            self.readers[w] = {}
        for r in reads:
            self.readers[r][E] = c
        return c

    def barrier(self):
        for s in STREAMS:
            kn = self.know[s]
            waits = []
            for F in SEMKEYS:
                c = self.cnt[F]
                if c and kn.get(F, 0) < c:
                    sn = self.snap[F][c // STEP[F] - 1]
                    for kk, vv in sn.items():
                        if kn.get(kk, 0) < vv:
                            kn[kk] = vv
                    kn[F] = c
                    waits.append((F, c))
            if waits:
                self.prog[s].append((None, (), {}, waits, None))

    def dma(self, q, out, in_, reads=(), writes=(), **kw):
        idx = self.qrr[q]
        self.qrr[q] = (idx + 1) % NQROT
        E = f"Q{q}{idx}"
        ed = [(E, self.cnt[E])] if self.cnt[E] else []
        return self.emit(E, 'dma_start', reads=reads, writes=writes,
                         extra_deps=ed, out=out, in_=in_, **kw)

    def replay(self, block):
        holders = {}

        def run(stream):
            def body(eng):
                holders[stream] = eng
                for fname, args, kwargs, waits, E in self.prog[stream]:
                    for F, c in waits:
                        eng.wait_ge(self.sems[F], c)
                    if fname is None:
                        continue
                    inst = getattr(eng, fname)(*args, **kwargs)
                    if E is not None:
                        inst.then_inc(self.sems[E], STEP[E])
            return body

        block.tensor(run('T'))
        block.vector(run('V'))
        block.scalar(run('S'))
        block.gpsimd(run('P'))
        block.sync(run('Y'))



def emit_attention(B, sb, sa, pa, pt, tau, heads, xnt, xntn, oall, oalln,
                   kscr, vscr, wq_t, masks, idt):
    q2 = [sb(f"q2{tau}{i}", [128, TILE], BF16, sa) for i in range(2)]
    kh = [sb(f"kh{tau}{i}", [128, SEQ], BF16, sa) for i in range(2)]
    vh = [sb(f"vh{tau}{i}", [128, 32, 128], BF16, sa) for i in range(2)]
    bsb = sb(f"bsb{tau}", [128, SEQ], F32, sa)
    scs = [sb(f"scs{tau}{i}", [128, SEQ], F32, sa) for i in range(2)]
    prb2 = [sb(f"prbb{tau}{i}", [128, SEQ], BF16, sa) for i in range(2)]
    ptb = sb(f"ptbuf{tau}", [128, 32, 256], BF16, sa)
    sums2 = [sb(f"sums{tau}{i}", [128, 8], F32, sa) for i in range(2)]
    asta = [sb(f"asta{tau}{i}", [128, 4], F32, sa) for i in range(2)]
    wqb = [sb(f"wq{tau}{i}", [128, DC, 128], BF16, sa) for i in range(2)]
    mskt = sb(f"mskt{tau}", [128, 8, 512], BF16, sa)
    B.dma('P', mskt[:],
          masks[tau * 8:(tau + 1) * 8].rearrange("s p k -> p s k"),
          writes=['mskt'])
    kwt = 512 * (4 * tau + 4)
    slope = slopes()

    def pre_dma(h):
        par = h % 2
        B.dma('P', wqb[par][:],
              wq_t[:, h * 128:(h + 1) * 128].rearrange("(c p) o -> p c o", p=128),
              writes=[f"wq{par}"])
        B.dma('Y', kh[par][:, 0:kwt], kscr[h, :, 0:kwt],
              reads=['kscr'], writes=[f"kh{par}"])
        B.dma('Y', vh[par][:, 0:kwt // 128, :],
              vscr[0:kwt, h * 128:(h + 1) * 128]
              .rearrange("(a p) e -> p a e", p=128),
              reads=['vscr'], writes=[f"vh{par}"])

    def pre_compute(h):
        par = h % 2
        for dc in range(DC):
            for qc in range(2):
                B.emit('T', 'matmul', pa[2 + qc][:], wqb[par][:, dc, :],
                       xnt[:, dc, qc * 512:(qc + 1) * 512],
                       start=(dc == 0), stop=(dc == DC - 1), inc=(dc == DC - 1),
                       reads=[f"wq{par}", xntn], writes=[f"pa{2 + qc}"])
        for qc in range(2):
            B.emit('S', 'copy', q2[par][:, qc * 512:(qc + 1) * 512],
                   pa[2 + qc][:], reads=[f"pa{2 + qc}"], writes=[f"q2{par}"])
        B.emit('P', 'iota', bsb[:, 0:kwt].bitcast(I32), [[1, kwt]],
               channel_multiplier=0, writes=['bsb'])
        B.emit('V', 'tensor_single_scalar', bsb[:, 0:kwt],
               bsb[:, 0:kwt].bitcast(I32), slope[h], ALU.mult,
               reads=['bsb'], writes=['bsb'])

    def cls_of(j):
        return 4 * tau + j // 2 + 1

    def stageA(h, j):
        par, sp = h % 2, j % 2
        kw = 512 * cls_of(j)
        for ch in range(cls_of(j)):
            bank = pa[ch % 2]
            bname = f"pa{ch % 2}"
            B.emit('T', 'matmul', bank[:], q2[par][:, j * 128:(j + 1) * 128],
                   kh[par][:, ch * 512:(ch + 1) * 512], start=True, stop=True,
                   reads=[f"q2{par}", f"kh{par}"], writes=[bname])
            B.emit('V', 'tensor_tensor', scs[sp][:, ch * 512:(ch + 1) * 512],
                   bank[:], bsb[:, ch * 512:(ch + 1) * 512], ALU.add,
                   reads=[bname, 'bsb'], writes=[f"scs{sp}"])
        B.emit('V', 'tensor_tensor', scs[sp][:, kw - 512:kw],
               scs[sp][:, kw - 512:kw], mskt[:, j, :], ALU.add,
               reads=[f"scs{sp}", 'mskt'], writes=[f"scs{sp}"])
        B.emit('V', 'reduce_max', asta[sp][:, 0:1], scs[sp][:, 0:kw], AX.X,
               negate=True, reads=[f"scs{sp}"], writes=[f"asta{sp}"])

    def stageB(h, j):
        sp = j % 2
        cls = cls_of(j)
        kw = 512 * cls
        for ch in range(cls):
            B.emit('S', 'activation', prb2[sp][:, ch * 512:(ch + 1) * 512],
                   scs[sp][:, ch * 512:(ch + 1) * 512], AF.Exp,
                   bias=asta[sp][:, 0:1], accum_out=sums2[sp][:, ch:ch + 1],
                   reads=[f"scs{sp}", f"asta{sp}"],
                   writes=[f"prbb{sp}", f"sums{sp}"])
        B.emit('V', 'reduce_sum', asta[sp][:, 1:2], sums2[sp][:, 0:cls], AX.X,
               reads=[f"sums{sp}"], writes=[f"asta{sp}"])
        B.emit('V', 'reciprocal', asta[sp][:, 2:3], asta[sp][:, 1:2],
               reads=[f"asta{sp}"], writes=[f"asta{sp}"])
        B.emit('V', 'tensor_scalar', prb2[sp][:, 0:kw], prb2[sp][:, 0:kw],
               asta[sp][:, 2:3], None, ALU.mult,
               reads=[f"prbb{sp}", f"asta{sp}"], writes=[f"prbb{sp}"])

    def stageC(h, j):
        par, sp = h % 2, j % 2
        m, pr = j % 2, j // 2
        nkt = 512 * cls_of(j) // 128
        for g8 in range((nkt + 7) // 8):
            gn = min(8, nkt - g8 * 8)
            bank = pt[g8 % 2]
            bname = f"pt{g8 % 2}"
            for k in range(gn):
                kt = g8 * 8 + k
                B.emit('T', 'transpose', bank[:, k * 128:(k + 1) * 128],
                       prb2[sp][:, kt * 128:(kt + 1) * 128], idt[:],
                       reads=[f"prbb{sp}", 'idt'], writes=[bname],
                       inc=(k == gn - 1))
            B.emit('S', 'copy',
                   ptb[:, g8 * 8:g8 * 8 + gn, m * 128:(m + 1) * 128],
                   bank[:, 0:gn * 128].rearrange("p (a b) -> p a b", b=128),
                   reads=[bname], writes=['ptbuf'])
        if m == 1:
            for kt in range(nkt):
                B.emit('T', 'matmul', pa[4][:, 0:256], vh[par][:, kt, :],
                       ptb[:, kt, :], start=(kt == 0), stop=(kt == nkt - 1), inc=(kt == nkt - 1),
                       reads=[f"vh{par}", 'ptbuf'], writes=['pa4'])
            B.emit('S', 'copy', oall[:, h, pr * 256:(pr + 1) * 256],
                   pa[4][:, 0:256], reads=['pa4'], writes=[oalln])

    pre_dma(0)
    pre_compute(0)
    total = heads * 8
    for t_ in range(total + 2):
        if t_ < total:
            stageA(*divmod(t_, 8))
        if 1 <= t_ <= total:
            stageB(*divmod(t_ - 1, 8))
        if 2 <= t_ <= total + 1:
            stageC(*divmod(t_ - 2, 8))
        if t_ < total:
            hA, jA = divmod(t_, 8)
            if jA == 5 and hA + 1 < heads:
                pre_dma(hA + 1)
            if jA == 7 and hA + 1 < heads:
                pre_compute(hA + 1)
    B.barrier()


def build_nc(cfg=None):
    cfg = dict(cfg or {})
    p1_cols = cfg.get('p1_cols', 8)
    tiles = cfg.get('tiles', [0, 1])
    heads = cfg.get('heads', H)
    experts = cfg.get('experts', NE)
    do_attn = cfg.get('do_attn', True)
    do_moe = cfg.get('do_moe', True)
    do_shared = cfg.get('do_shared', True)
    do_p1 = cfg.get('do_p1', True)

    nc = bass.Bass("TRN2", target_bir_lowering=False)

    dr = {}
    def din(name, shape, dt=F32):
        dr[name] = nc.dram_tensor(name, shape, dt, kind="ExternalInput").ap()
        return dr[name]

    xseq = din("xseq", [SEQ, D])
    xown = din("xown", [2048, D])
    wq_t = din("wq_t", [D, D])
    wk_t = din("wk_t", [D, D])
    wv_t = din("wv_t", [D, D])
    wo_t = din("wo_t", [D, D])
    gw_t = din("gw_t", [NE, D, FM])
    uw_t = din("uw_t", [NE, D, FM])
    dw_t = din("dw_t", [NE, FM, D])
    sg_t = din("sg_t", [D, FSH])
    su_t = din("su_t", [D, FSH])
    sd_t = din("sd_t", [FSH, D])
    rt = din("rt", [D, NE])
    ebi = din("eb", [128, NE])
    ident_in = din("ident", [128, 128])
    masks = din("masks", [NSLOT, 128, 512])

    out_d = nc.dram_tensor("out", [DC, 128, 2048], F32, kind="ExternalOutput").ap()
    kscr = nc.dram_tensor("kscr", [H, DH, SEQ], BF16, kind="Internal").ap()
    vscr = nc.dram_tensor("vscr", [SEQ, D], BF16, kind="Internal").ap()

    with contextlib.ExitStack() as top:
        B = Bld(nc, top)
        sb = lambda n, shp, dt, st=top, side=None: st.enter_context(nc.sbuf_tensor(n, shp, dt, side=side))
        pa = [top.enter_context(nc.psum_tensor(f"pa{i}", [128, 512], F32))
              for i in range(6)]
        pt = [top.enter_context(nc.psum_tensor(f"pt{i}", [128, 1024], BF16))
              for i in range(2)]

        idt = sb("idt", [128, 128], BF16)
        idtf = sb("idtf", [128, 128], F32)
        ebbc = sb("ebbc", [128, NE], F32)
        B.dma('P', idt[:], ident_in, writes=['idt'])
        B.dma('Y', idtf[:], ident_in, writes=['idtf'])
        B.dma('Y', ebbc[:], ebi, writes=['ebbc'])

        def emit_ln(stx, par, x_rows_ap, dst, dst_name, dst_toff, bufs):
            xt = bufs['xt'][par]; xtn = f"xt{par}"
            sq = bufs['sq'][par]; sqn = f"sq{par}"
            xn = bufs['xn'][par]; xnn = f"xn{par}"
            stt_ = bufs['st'][par]; stn = f"lst{par}"
            B.dma('Y', xt[:], x_rows_ap, writes=[xtn])
            B.emit('V', 'reduce_sum', stt_[:, 0:1], xt[:], AX.X,
                   reads=[xtn], writes=[stn])
            B.emit('S', 'activation', sq[:], xt[:], AF.Square,
                   accum_out=stt_[:, 2:3], reads=[xtn], writes=[sqn, stn])
            B.emit('V', 'tensor_scalar', stt_[:, 1:2], stt_[:, 0:1],
                   -1.0 / D, None, ALU.mult, reads=[stn], writes=[stn])
            B.emit('V', 'tensor_scalar', stt_[:, 3:4], stt_[:, 2:3],
                   1.0 / D, None, ALU.mult, reads=[stn], writes=[stn])
            B.emit('V', 'tensor_tensor', stt_[:, 4:5], stt_[:, 1:2], stt_[:, 1:2],
                   ALU.mult, reads=[stn], writes=[stn])
            B.emit('V', 'tensor_tensor', stt_[:, 5:6], stt_[:, 3:4], stt_[:, 4:5],
                   ALU.subtract, reads=[stn], writes=[stn])
            B.emit('V', 'tensor_scalar', stt_[:, 5:6], stt_[:, 5:6], 1e-5, None,
                   ALU.add, reads=[stn], writes=[stn])
            B.emit('S', 'activation', stt_[:, 6:7], stt_[:, 5:6], AF.Sqrt,
                   reads=[stn], writes=[stn])
            B.emit('V', 'reciprocal', stt_[:, 7:8], stt_[:, 6:7],
                   reads=[stn], writes=[stn])
            B.emit('V', 'tensor_scalar', xn[:], xt[:], stt_[:, 1:2], stt_[:, 7:8],
                   ALU.add, ALU.mult, reads=[xtn, stn], writes=[xnn])
            for g8 in range(2):
                ptb = pt[g8]
                ptn = f"pt{g8}"
                for k in range(8):
                    dc = g8 * 8 + k
                    B.emit('T', 'transpose', ptb[:, k * 128:(k + 1) * 128],
                           xn[:, dc * 128:(dc + 1) * 128], idt[:],
                           reads=[xnn, 'idt'], writes=[ptn], inc=(k == 7))
                B.emit('V', 'tensor_copy',
                       dst[:, g8 * 8:(g8 + 1) * 8, dst_toff:dst_toff + 128],
                       ptb[:].rearrange("p (a b) -> p a b", a=8),
                       reads=[ptn], writes=[dst_name])

        if do_p1:
            with contextlib.ExitStack() as s1:
                wk = sb("wk", [128, DC, D], BF16, s1)
                wv = sb("wv", [128, DC, D], BF16, s1)
                xnf = sb("xnf", [128, DC, 512], BF16, s1)
                kcol = sb("kcol", [128, H, 512], BF16, s1)
                bufs = {
                    'xt': [sb(f"p1xt{i}", [128, D], F32, s1) for i in range(2)],
                    'sq': [sb(f"p1sq{i}", [128, D], BF16, s1) for i in range(2)],
                    'xn': [sb(f"p1xn{i}", [128, D], BF16, s1) for i in range(2)],
                    'st': [sb(f"p1st{i}", [128, 8], F32, s1) for i in range(2)],
                }
                vst = [sb(f"vst{i}", [128, D], BF16, s1) for i in range(2)]
                B.dma('P', wk[:], wk_t.rearrange("(c p) o -> p c o", p=128),
                      writes=['wk'])
                B.dma('P', wv[:], wv_t.rearrange("(c p) o -> p c o", p=128),
                      writes=['wv'])
                for col in range(p1_cols):
                    for tt in range(4):
                        r0 = col * 512 + tt * 128
                        emit_ln(s1, tt % 2, xseq[r0:r0 + 128, :],
                                xnf, 'xnf', tt * 128, bufs)
                    for h in range(H):
                        bank = pa[h % 2]
                        bname = f"pa{h % 2}"
                        for dc in range(DC):
                            B.emit('T', 'matmul', bank[:],
                                   wk[:, dc, h * 128:(h + 1) * 128],
                                   xnf[:, dc, :],
                                   start=(dc == 0), stop=(dc == DC - 1), inc=(dc == DC - 1),
                                   reads=['wk', 'xnf'], writes=[bname])
                        B.emit('S', 'mul', kcol[:, h, :], bank[:], KSCALE,
                               reads=[bname], writes=['kcol'])
                    B.dma('Y',
                          kscr[:, :, col * 512:(col + 1) * 512]
                          .rearrange("h p k -> p h k"),
                          kcol[:], reads=['kcol'], writes=['kscr'])
                    for tt in range(4):
                        vpar = tt % 2
                        vb = vst[vpar]
                        vbn = f"vst{vpar}"
                        for ec in range(4):
                            bank = pa[2 + ec % 2]
                            bname = f"pa{2 + ec % 2}"
                            for dc in range(DC):
                                B.emit('T', 'matmul', bank[:],
                                       xnf[:, dc, tt * 128:(tt + 1) * 128],
                                       wv[:, dc, ec * 512:(ec + 1) * 512],
                                       start=(dc == 0), stop=(dc == DC - 1), inc=(dc == DC - 1),
                                       reads=['xnf', 'wv'], writes=[bname])
                            B.emit('V', 'tensor_copy',
                                   vb[:, ec * 512:(ec + 1) * 512], bank[:],
                                   reads=[bname], writes=[vbn])
                        r0 = col * 512 + tt * 128
                        B.dma('Y', vscr[r0:r0 + 128, :], vb[:],
                              reads=[vbn], writes=['vscr'])
                B.barrier()

        acc_live = False
        for tau in tiles:
            with contextlib.ExitStack() as s2:
                xnt = sb(f"xnt{tau}", [128, DC, TILE], BF16, s2)
                xntn = f"xnt{tau}"
                wall = sb(f"wall{tau}", [128, 8, NE], F32, s2)
                walln = f"wall{tau}"
                with contextlib.ExitStack() as sl:
                    bufs = {
                        'xt': [sb(f"t{tau}xt{i}", [128, D], F32, sl) for i in range(2)],
                        'sq': [sb(f"t{tau}sq{i}", [128, D], BF16, sl) for i in range(2)],
                        'xn': [sb(f"t{tau}xn{i}", [128, D], BF16, sl) for i in range(2)],
                        'st': [sb(f"t{tau}st{i}", [128, 8], F32, sl) for i in range(2)],
                    }
                    for ts in range(8):
                        r0 = tau * 1024 + ts * 128
                        emit_ln(sl, ts % 2, xown[r0:r0 + 128, :],
                                xnt, xntn, ts * 128, bufs)
                    B.barrier()

                with contextlib.ExitStack() as sr:
                    rtb = sb(f"rtb{tau}", [128, DC, NE], BF16, sr)
                    lg = sb(f"lg{tau}", [128, NE], F32, sr)
                    sc = sb(f"sc{tau}", [128, NE], F32, sr)
                    sfc = sb(f"sfc{tau}", [128, NE], F32, sr)
                    grp6 = sb(f"grp6{tau}", [128, 4, 6], F32, sr)
                    gs = sb(f"gs{tau}", [128, 4], F32, sr)
                    gsc = sb(f"gsc{tau}", [128, 12], F32, sr)
                    is1 = sb(f"is1{tau}", [128, 4], F32, sr)
                    is2 = sb(f"is2{tau}", [128, 4], F32, sr)
                    gm = sb(f"gm{tau}", [128, 4], F32, sr)
                    mk = sb(f"mk{tau}", [128, NE], F32, sr)
                    selm = sb(f"selm{tau}", [128, NE], F32, sr)
                    isr = sb(f"isr{tau}", [128, NE], F32, sr)
                    wraw = sb(f"wraw{tau}", [128, NE], F32, sr)
                    B.dma('P', rtb[:], rt.rearrange("(c p) n -> p c n", p=128),
                          writes=['rtb'])
                    for ts in range(8):
                        bank = pa[0]
                        for dc in range(DC):
                            B.emit('T', 'matmul', bank[:, 0:NE],
                                   xnt[:, dc, ts * 128:(ts + 1) * 128],
                                   rtb[:, dc, :],
                                   start=(dc == 0), stop=(dc == DC - 1), inc=(dc == DC - 1),
                                   reads=[xntn, 'rtb'], writes=['pa0'])
                        B.emit('V', 'tensor_copy', lg[:], bank[:, 0:NE],
                               reads=['pa0'], writes=['lg'])
                        B.emit('S', 'activation', sc[:], lg[:], AF.Sigmoid,
                               reads=['lg'], writes=['sc'])
                        B.emit('V', 'tensor_tensor', sfc[:], sc[:], ebbc[:],
                               ALU.add, reads=['sc', 'ebbc'], writes=['sfc'])
                        pairs = [(0, 1), (0, 2), (0, 3), (1, 2), (1, 3), (2, 3)]
                        for pi, (a, b) in enumerate(pairs):
                            B.emit('V', 'tensor_tensor', grp6[:, :, pi],
                                   sfc[:].rearrange("p (g e) -> p g e", e=4)[:, :, a],
                                   sfc[:].rearrange("p (g e) -> p g e", e=4)[:, :, b],
                                   ALU.add, reads=['sfc'], writes=['grp6'])
                        B.emit('V', 'reduce_max', gs[:].rearrange("p (g o) -> p g o", o=1),
                               grp6[:], AX.X, reads=['grp6'], writes=['gs'])
                        B.emit('V', 'reduce_max', gsc[:, 0:1], gs[:], AX.X,
                               reads=['gs'], writes=['gsc'])
                        B.emit('V', 'tensor_scalar', is1[:], gs[:], gsc[:, 0:1],
                               None, ALU.is_ge, reads=['gs', 'gsc'], writes=['is1'])
                        B.emit('V', 'scalar_tensor_tensor', gm[:], is1[:], NEG,
                               gs[:], ALU.mult, ALU.add,
                               reads=['is1', 'gs'], writes=['gm'])
                        B.emit('V', 'reduce_max', gsc[:, 1:2], gm[:], AX.X,
                               reads=['gm'], writes=['gsc'])
                        B.emit('V', 'tensor_scalar', is2[:], gm[:], gsc[:, 1:2],
                               None, ALU.is_ge, reads=['gm', 'gsc'], writes=['is2'])
                        B.emit('V', 'tensor_tensor', gm[:], is1[:], is2[:],
                               ALU.add, reads=['is1', 'is2'], writes=['gm'])
                        B.emit('V', 'tensor_scalar', gm[:], gm[:], -1.0, -NEG,
                               ALU.add, ALU.mult, reads=['gm'], writes=['gm'])
                        for g in range(4):
                            B.emit('V', 'tensor_scalar', mk[:, 4 * g:4 * g + 4],
                                   sfc[:, 4 * g:4 * g + 4], gm[:, g:g + 1], None,
                                   ALU.add, reads=['sfc', 'gm'], writes=['mk'])
                        B.emit('V', 'memset', selm[:], 0.0, writes=['selm'])
                        for r in range(4):
                            B.emit('V', 'reduce_max', gsc[:, 2:3], mk[:], AX.X,
                                   reads=['mk'], writes=['gsc'])
                            B.emit('V', 'tensor_scalar', isr[:], mk[:],
                                   gsc[:, 2:3], None, ALU.is_ge,
                                   reads=['mk', 'gsc'], writes=['isr'])
                            B.emit('V', 'tensor_tensor', selm[:], selm[:], isr[:],
                                   ALU.max, reads=['selm', 'isr'], writes=['selm'])
                            if r < 3:
                                B.emit('V', 'scalar_tensor_tensor', mk[:], isr[:],
                                       NEG, mk[:], ALU.mult, ALU.add,
                                       reads=['isr', 'mk'], writes=['mk'])
                        B.emit('V', 'tensor_tensor', wraw[:], sc[:], selm[:],
                               ALU.mult, reads=['sc', 'selm'], writes=['wraw'])
                        B.emit('V', 'reduce_sum', gsc[:, 3:4], wraw[:], AX.X,
                               reads=['wraw'], writes=['gsc'])
                        B.emit('V', 'tensor_scalar', gsc[:, 4:5], gsc[:, 3:4],
                               1e-20, None, ALU.add, reads=['gsc'], writes=['gsc'])
                        B.emit('V', 'reciprocal', gsc[:, 5:6], gsc[:, 4:5],
                               reads=['gsc'], writes=['gsc'])
                        B.emit('V', 'tensor_scalar', wall[:, ts, :], wraw[:],
                               gsc[:, 5:6], 2.5, ALU.mult, ALU.mult,
                               reads=['wraw', 'gsc'], writes=[walln])
                    B.barrier()

                s2b = s2.enter_context(contextlib.ExitStack())
                oall = sb(f"oall{tau}", [128, H, TILE], BF16, s2b)
                oalln = f"oall{tau}"
                if do_attn:
                    B.emit('V', 'memset', oall[:], 0.0, writes=[oalln])
                    with contextlib.ExitStack() as sa:
                        emit_attention(B, sb, sa, pa, pt, tau, heads, xnt,
                                       xntn, oall, oalln, kscr, vscr, wq_t,
                                       masks, idt)

                acc = sb(f"acc{tau}", [128, DC, TILE], F32, s2, side="right")
                accn = f"acc{tau}"
                with contextlib.ExitStack() as sw:
                    xtw = [sb(f"w{tau}xt{i}", [128, D], F32, sw) for i in range(2)]
                    wob = [sb(f"wo{tau}{i}", [128, DC, 128], BF16, sw)
                           for i in range(2)]
                    for ts in range(8):
                        par = ts % 2
                        r0 = tau * 1024 + ts * 128
                        B.dma('Y', xtw[par][:], xown[r0:r0 + 128, :],
                              writes=[f"wxt{par}"])
                        for g4 in range(4):
                            bank = pa[3 + g4 % 3]
                            bname = f"pa{3 + g4 % 3}"
                            for k in range(4):
                                dc = g4 * 4 + k
                                B.emit('T', 'transpose',
                                       bank[:, k * 128:(k + 1) * 128],
                                       xtw[par][:, dc * 128:(dc + 1) * 128],
                                       idtf[:],
                                       reads=[f"wxt{par}", 'idtf'], writes=[bname],
                                       inc=(k == 3))
                            B.emit('V', 'tensor_copy',
                                   acc[:, g4 * 4:(g4 + 1) * 4, ts * 128:(ts + 1) * 128],
                                   bank[:].rearrange("p (a b) -> p a b", a=4),
                                   reads=[bname], writes=[accn])
                    if do_attn:
                        for oc in range(DC):
                            wpar = oc % 2
                            if oc == 0:
                                B.dma('P', wob[0][:],
                                      wo_t[:, 0:128].rearrange("(c p) o -> p c o", p=128),
                                      writes=['wo0'])
                            if oc + 1 < DC:
                                B.dma('P', wob[(oc + 1) % 2][:],
                                      wo_t[:, (oc + 1) * 128:(oc + 2) * 128]
                                      .rearrange("(c p) o -> p c o", p=128),
                                      writes=[f"wo{(oc + 1) % 2}"])
                            for qc in range(2):
                                bank = pa[qc]
                                bname = f"pa{qc}"
                                for ec in range(H):
                                    B.emit('T', 'matmul', bank[:],
                                           wob[wpar][:, ec, :],
                                           oall[:, ec, qc * 512:(qc + 1) * 512],
                                           start=(ec == 0), stop=(ec == H - 1), inc=(ec == H - 1),
                                           reads=[f"wo{wpar}", oalln],
                                           writes=[bname])
                                B.emit('V', 'tensor_tensor',
                                       acc[:, oc, qc * 512:(qc + 1) * 512],
                                       bank[:],
                                       acc[:, oc, qc * 512:(qc + 1) * 512],
                                       ALU.add,
                                       reads=[bname, accn], writes=[accn])
                    B.barrier()

                s2b.close()

                if do_moe:
                    with contextlib.ExitStack() as sm:
                        wgb = [sb(f"wg{tau}{i}", [128, DC, 512], BF16, sm)
                               for i in range(2)]
                        wub = [sb(f"wu{tau}{i}", [128, DC, 512], BF16, sm)
                               for i in range(2)]
                        dwb = [sb(f"dw{tau}{i}", [128, 8, 128], BF16, sm)
                               for i in range(2)]
                        act = sb(f"act{tau}", [128, 8, TILE], BF16, sm)
                        sg_ = [sb(f"sg_{tau}{i}", [128, 512], BF16, sm)
                               for i in range(2)]
                        ast = [sb(f"ast{tau}{i}", [128, 512], BF16, sm)
                               for i in range(2)]

                        def gu_dma(n, fc, par):
                            B.dma('P', wgb[par][:],
                                  gw_t[n, :, fc * 512:(fc + 1) * 512]
                                  .rearrange("(c p) f -> p c f", p=128),
                                  writes=[f"wg{par}"])
                            B.dma('P', wub[par][:],
                                  uw_t[n, :, fc * 512:(fc + 1) * 512]
                                  .rearrange("(c p) f -> p c f", p=128),
                                  writes=[f"wu{par}"])

                        def dw_dma(n, hc, par):
                            B.dma('P', dwb[par][:],
                                  dw_t[n, :, hc * 128:(hc + 1) * 128]
                                  .rearrange("(c p) h2 -> p c h2", p=128),
                                  writes=[f"dw{par}"])

                        gu_dma(0, 0, 0)
                        for ni in range(experts):
                            n = ni
                            def moe_transp(fc, ts):
                                spar = ts % 2
                                bank = pt[ts % 2]
                                bname = f"pt{ts % 2}"
                                for ft in range(4):
                                    B.emit('T', 'transpose',
                                           bank[:, ft * 128:(ft + 1) * 128],
                                           ast[spar][:, ft * 128:(ft + 1) * 128],
                                           idt[:],
                                           reads=[f"ast{spar}", 'idt'],
                                           writes=[bname], inc=(ft == 3))
                                B.emit('V', 'tensor_copy',
                                       act[:, fc * 4:fc * 4 + 4,
                                           ts * 128:(ts + 1) * 128],
                                       bank[:, 0:512].rearrange(
                                           "p (a b) -> p a b", a=4),
                                       reads=[bname], writes=['act'])

                            for fc in range(2):
                                gpar = fc % 2
                                if fc + 1 < 2:
                                    gu_dma(n, fc + 1, (fc + 1) % 2)
                                elif ni + 1 < experts:
                                    gu_dma(ni + 1, 0, 0)
                                for ts in range(9):
                                    if ts < 8:
                                        pg, pu = pa[ts % 2], pa[2 + ts % 2]
                                        pgn = f"pa{ts % 2}"
                                        pun = f"pa{2 + ts % 2}"
                                        for dc in range(DC):
                                            B.emit('T', 'matmul', pg[:],
                                                   xnt[:, dc, ts * 128:(ts + 1) * 128],
                                                   wgb[gpar][:, dc, :],
                                                   start=(dc == 0), stop=(dc == DC - 1), inc=(dc == DC - 1),
                                                   reads=[xntn, f"wg{gpar}"],
                                                   writes=[pgn])
                                            B.emit('T', 'matmul', pu[:],
                                                   xnt[:, dc, ts * 128:(ts + 1) * 128],
                                                   wub[gpar][:, dc, :],
                                                   start=(dc == 0), stop=(dc == DC - 1), inc=(dc == DC - 1),
                                                   reads=[xntn, f"wu{gpar}"],
                                                   writes=[pun])
                                        spar = ts % 2
                                        B.emit('S', 'activation', sg_[spar][:], pg[:],
                                               AF.Silu, reads=[pgn],
                                               writes=[f"sg_{spar}"])
                                        B.emit('V', 'scalar_tensor_tensor',
                                               ast[spar][:], pu[:],
                                               wall[:, ts, n:n + 1], sg_[spar][:],
                                               ALU.mult, ALU.mult,
                                               reads=[pun, walln, f"sg_{spar}"],
                                               writes=[f"ast{spar}"])
                                    if ts >= 1:
                                        moe_transp(fc, ts - 1)
                            dw_dma(n, 0, 0)
                            for hc in range(DC):
                                dpar = hc % 2
                                if hc + 1 < DC:
                                    dw_dma(n, hc + 1, (hc + 1) % 2)
                                elif ni + 1 < experts:
                                    pass
                                for tc in range(2):
                                    bank = pa[4 + tc]
                                    bname = f"pa{4 + tc}"
                                    for fcc in range(8):
                                        B.emit('T', 'matmul', bank[:],
                                               dwb[dpar][:, fcc, :],
                                               act[:, fcc, tc * 512:(tc + 1) * 512],
                                               start=(fcc == 0), stop=(fcc == 7), inc=(fcc == 7),
                                               reads=[f"dw{dpar}", 'act'],
                                               writes=[bname])
                                    B.emit('V', 'tensor_tensor',
                                           acc[:, hc, tc * 512:(tc + 1) * 512],
                                           bank[:],
                                           acc[:, hc, tc * 512:(tc + 1) * 512],
                                           ALU.add,
                                           reads=[bname, accn], writes=[accn])
                        B.barrier()

                if do_shared:
                    with contextlib.ExitStack() as ss:
                        acts = sb(f"acts{tau}", [128, 16, TILE], BF16, ss)
                        sgb = [sb(f"sgb{tau}{i}", [128, DC, 128], BF16, ss)
                               for i in range(2)]
                        sub = [sb(f"sub{tau}{i}", [128, DC, 128], BF16, ss)
                               for i in range(2)]
                        sdb = [sb(f"sdb{tau}{i}", [128, 16, 128], BF16, ss)
                               for i in range(2)]
                        sgs = [sb(f"sgs{tau}{i}", [128, 512], BF16, ss)
                               for i in range(2)]

                        def sh_dma(fc, par):
                            B.dma('P', sgb[par][:],
                                  sg_t[:, fc * 128:(fc + 1) * 128]
                                  .rearrange("(c p) f -> p c f", p=128),
                                  writes=[f"sgb{par}"])
                            B.dma('P', sub[par][:],
                                  su_t[:, fc * 128:(fc + 1) * 128]
                                  .rearrange("(c p) f -> p c f", p=128),
                                  writes=[f"sub{par}"])

                        def sd_dma(hc, par):
                            B.dma('P', sdb[par][:],
                                  sd_t[:, hc * 128:(hc + 1) * 128]
                                  .rearrange("(c p) h2 -> p c h2", p=128),
                                  writes=[f"sdb{par}"])

                        sh_dma(0, 0)
                        for fc in range(16):
                            par = fc % 2
                            if fc + 1 < 16:
                                sh_dma(fc + 1, (fc + 1) % 2)
                            for tc in range(2):
                                pg, pu = pa[tc], pa[2 + tc]
                                pgn, pun = f"pa{tc}", f"pa{2 + tc}"
                                for dc in range(DC):
                                    B.emit('T', 'matmul', pg[:],
                                           sgb[par][:, dc, :],
                                           xnt[:, dc, tc * 512:(tc + 1) * 512],
                                           start=(dc == 0), stop=(dc == DC - 1), inc=(dc == DC - 1),
                                           reads=[f"sgb{par}", xntn], writes=[pgn])
                                    B.emit('T', 'matmul', pu[:],
                                           sub[par][:, dc, :],
                                           xnt[:, dc, tc * 512:(tc + 1) * 512],
                                           start=(dc == 0), stop=(dc == DC - 1), inc=(dc == DC - 1),
                                           reads=[f"sub{par}", xntn], writes=[pun])
                                B.emit('S', 'activation', sgs[tc][:], pg[:],
                                       AF.Silu, reads=[pgn], writes=[f"sgs{tc}"])
                                B.emit('V', 'scalar_tensor_tensor',
                                       acts[:, fc, tc * 512:(tc + 1) * 512],
                                       pu[:], 1.0, sgs[tc][:],
                                       ALU.mult, ALU.mult,
                                       reads=[pun, f"sgs{tc}"], writes=['acts'])
                        sd_dma(0, 0)
                        for hc in range(DC):
                            par = hc % 2
                            if hc + 1 < DC:
                                sd_dma(hc + 1, (hc + 1) % 2)
                            for tc in range(2):
                                bank = pa[4 + tc]
                                bname = f"pa{4 + tc}"
                                for fcc in range(16):
                                    B.emit('T', 'matmul', bank[:],
                                           sdb[par][:, fcc, :],
                                           acts[:, fcc, tc * 512:(tc + 1) * 512],
                                           start=(fcc == 0), stop=(fcc == 15), inc=(fcc == 15),
                                           reads=[f"sdb{par}", 'acts'],
                                           writes=[bname])
                                B.emit('V', 'tensor_tensor',
                                       acc[:, hc, tc * 512:(tc + 1) * 512],
                                       bank[:],
                                       acc[:, hc, tc * 512:(tc + 1) * 512],
                                       ALU.add,
                                       reads=[bname, accn], writes=[accn])
                        B.barrier()

                B.dma('Y',
                      out_d[:, :, tau * 1024:(tau + 1) * 1024]
                      .rearrange("c p t -> p c t"),
                      acc[:], reads=[accn], writes=['out'])
                B.barrier()

        fin = [(k, B.cnt[k]) for k in SEMKEYS
               if k.startswith('Q') and B.cnt[k]]
        B.prog['Y'].append((None, (), {}, fin, None))

        with nc.Block() as block:
            B.replay(block)

    return nc



def _prep_core_inputs(core, x, wq, wk, wv, wo, gwt, uwt, dwt, sgt, sut, sdt,
                      rtw, ebv, idn, maskcache={}):
    b = core >> 1
    half = core & 1
    xb = np.ascontiguousarray(x[b])
    rows = []
    for j in range(NSLOT):
        g = slot_global(j, half)
        rows.append(xb[128 * g:128 * (g + 1)])
    xo = np.ascontiguousarray(np.concatenate(rows, axis=0))
    if half not in maskcache:
        mk = np.zeros((NSLOT, 128, 512), dtype=np.float32)
        for j in range(NSLOT):
            g = slot_global(j, half)
            cls = j // 2 + 1
            q = 128 * g + np.arange(128)[:, None]
            kk = 512 * (cls - 1) + np.arange(512)[None, :]
            mk[j] = np.where(kk <= q, 0.0, NEG).astype(np.float32)
        maskcache[half] = mk
    return {
        "xseq": xb, "xown": xo,
        "wq_t": wq, "wk_t": wk, "wv_t": wv, "wo_t": wo,
        "gw_t": gwt, "uw_t": uwt, "dw_t": dwt,
        "sg_t": sgt, "su_t": sut, "sd_t": sdt,
        "rt": rtw, "eb": ebv, "ident": idn, "masks": maskcache[half],
    }


def kernel(L0_attn_x, L0_attn_Wq, L0_attn_Wk, L0_attn_Wv, L0_attn_Wo,
           L0_ffn_gate_weight, L0_ffn_e_score_correction_bias,
           L0_ffn_gate_w, L0_ffn_up_w, L0_ffn_down_w,
           L0_ffn_sh_gate_w, L0_ffn_sh_up_w, L0_ffn_sh_down_w):
    x = np.asarray(L0_attn_x, dtype=np.float32)
    T = lambda a: np.ascontiguousarray(np.asarray(a, np.float32).T)
    wq, wk = T(L0_attn_Wq), T(L0_attn_Wk)
    wv, wo = T(L0_attn_Wv), T(L0_attn_Wo)
    gwt = np.ascontiguousarray(np.asarray(L0_ffn_gate_w, np.float32).transpose(0, 2, 1))
    uwt = np.ascontiguousarray(np.asarray(L0_ffn_up_w, np.float32).transpose(0, 2, 1))
    dwt = np.ascontiguousarray(np.asarray(L0_ffn_down_w, np.float32).transpose(0, 2, 1))
    sgt, sut = T(L0_ffn_sh_gate_w), T(L0_ffn_sh_up_w)
    sdt = T(L0_ffn_sh_down_w)
    rtw = T(L0_ffn_gate_weight)
    ebv = np.ascontiguousarray(np.broadcast_to(
        np.asarray(L0_ffn_e_score_correction_bias, np.float32).reshape(1, NE),
        (128, NE)))
    idn = np.eye(128, dtype=np.float32)

    nc = build_nc()
    in_maps = [
        _prep_core_inputs(c, x, wq, wk, wv, wo, gwt, uwt, dwt, sgt, sut, sdt,
                          rtw, ebv, idn)
        for c in range(8)
    ]
    res = run_bass_kernel_spmd(nc, in_maps, core_ids=list(range(8)))

    out = np.empty((B_SZ, SEQ, D), dtype=np.float32)
    for c in range(8):
        b, half = c >> 1, c & 1
        arr = np.asarray(res.results[c]["out"])
        rows = arr.reshape(D, 2048).T
        for j in range(NSLOT):
            g = slot_global(j, half)
            out[b, 128 * g:128 * (g + 1), :] = rows[128 * j:128 * (j + 1)]
    return out
